# version 08
# baseline (speedup 1.0000x reference)
import numpy as np
import ml_dtypes
import concourse.bass as bass
import concourse.mybir as mybir
from concourse.bass_utils import run_bass_kernel_spmd

F32 = mybir.dt.float32
BF16 = mybir.dt.bfloat16
FP8 = mybir.dt.float8e4
AF = mybir.ActivationFunctionType
ALU = mybir.AluOpType
AX = mybir.AxisListType

MAGIC = 1.5 * 2.0 ** 23
EXPB = -30.0
T = 2048
D = 1024
IC = 8
INNER = 4096
KC = 32
NTT = 16
NQC = 4
NU = 16
NHP = 8
NW = float(INNER * D)

ENGS = ("sync", "tensor", "scalar", "vector", "gpsimd")


class Sched:
    def __init__(self, nc, sems):
        self.nc = nc
        self.sems = sems
        self.cnt = {k: 0 for k in sems}
        self.q = {e: [] for e in ENGS}

    def put(self, eng, fn, *args, inc=None, n=1):
        if inc is None:
            self.q[eng].append((fn, args, None, 0))
            return None
        self.q[eng].append((fn, args, self.sems[inc], n))
        self.cnt[inc] += n
        return self.cnt[inc]

    def wait(self, eng, key, val):
        if val is None or val <= 0:
            return
        sem = self.sems[key]

        def w(e, _s=sem, _v=val):
            e.wait_ge(_s, _v)

        self.q[eng].append((w, (), None, 0))

    def dma(self, eng, key, out, in_, transpose=False):
        if transpose:
            def fn(e, o, i):
                return e.dma_start_transpose(o, i)
        else:
            def fn(e, o, i):
                return e.dma_start(out=o, in_=i)
        return self.put(eng, fn, out, in_, inc=key, n=16)

    def flush(self):
        qs, self.q = self.q, {e: [] for e in ENGS}
        with self.nc.Block() as blk:
            for eng in ENGS:
                if not qs[eng]:
                    continue

                def run(e, _lst=qs[eng]):
                    for fn, args, sem, n in _lst:
                        ins = fn(e, *args)
                        if sem is not None:
                            ins.then_inc(sem, n)

                getattr(blk, eng)(run)


def _mm(e, out, l, r, st, sp, tp):
    return e.matmul(out, l, r, start=st, stop=sp, tile_position=tp)


def _act(e, out, in_, f, b, s):
    if b is None and s is None:
        return e.activation(out, in_, f)
    if b is None:
        return e.activation(out, in_, f, scale=s)
    if s is None:
        return e.activation(out, in_, f, bias=b)
    return e.activation(out, in_, f, bias=b, scale=s)


def _cp(e, o, i):
    return e.copy(o, i)


def _tt(e, o, a, b, op):
    return e.tensor_tensor(o, a, b, op=op)


def _ts(e, o, a, s1, s2, o0, o1):
    if s2 is None:
        return e.tensor_single_scalar(o, a, s1, op=o0)
    return e.tensor_scalar(o, a, s1, s2, op0=o0, op1=o1)


def _stt(e, o, a, sc, b, o0, o1):
    return e.scalar_tensor_tensor(o, a, sc, b, op0=o0, op1=o1)


def _red(e, o, i, op, absv):
    return e.tensor_reduce(o, i, axis=AX.X, op=op, apply_absolute_value=absv)


def _recip(e, o, i):
    return e.reciprocal(o, i)


def _mset(e, ap, v):
    return e.memset(ap, v)


def build_program():
    nc = bass.Bass("TRN2", target_bir_lowering=False)

    x_d = nc.dram_tensor("x", [2 * T, D], F32, kind="ExternalInput").ap()
    wqT_d = nc.dram_tensor("wqT", [D, D], F32, kind="ExternalInput").ap()
    wkT_d = nc.dram_tensor("wkT", [D, D], F32, kind="ExternalInput").ap()
    wvT_d = nc.dram_tensor("wvT", [D, D], F32, kind="ExternalInput").ap()
    wgT_d = nc.dram_tensor("wgT", [D, INNER], F32, kind="ExternalInput").ap()
    wuT_d = nc.dram_tensor("wuT", [D, INNER], F32, kind="ExternalInput").ap()
    wdT_d = nc.dram_tensor("wdT", [INNER, D], F32, kind="ExternalInput").ap()
    cc_d = nc.dram_tensor("cc", [128, 16], F32, kind="ExternalInput").ap()
    perm_d = nc.dram_tensor("perm", [128, 128], BF16, kind="ExternalInput").ap()
    ropeC_d = nc.dram_tensor("ropeC", [128, T], BF16, kind="ExternalInput").ap()
    ropeD_d = nc.dram_tensor("ropeD", [128, T], BF16, kind="ExternalInput").ap()
    out_d = nc.dram_tensor("out", [2 * T, D], F32, kind="ExternalOutput").ap()

    xn_s = nc.dram_tensor("xn_s", [T, D], BF16, kind="Internal").ap()
    xq_s = nc.dram_tensor("xq_s", [2 * T, D], BF16, kind="Internal").ap()
    O_s = nc.dram_tensor("O_s", [2 * 1040, T], BF16, kind="Internal").ap()
    wgq_s = nc.dram_tensor("wgq_s", [D, INNER], FP8, kind="Internal").ap()
    wuq_s = nc.dram_tensor("wuq_s", [D, INNER], FP8, kind="Internal").ap()
    wdq_s = nc.dram_tensor("wdq_s", [INNER, D], BF16, kind="Internal").ap()
    hq_s = nc.dram_tensor("hq_s", [2 * T, INNER], BF16, kind="Internal").ap()
    hq_s2 = nc.dram_tensor("hq_s2", [KC, 128, 2 * T], BF16, kind="Internal").ap()

    SBUF_BASE = 0x4000

    def sbt(name, shape, dt, off):
        return nc.alloc_sbuf_tensor_at(name, shape, dt, offset=SBUF_BASE + off).ap()

    wbufQKV = sbt("wbufQKV", [128, IC, D], BF16, 0x00000)
    Psb = sbt("Psb", [128, 2, 1024], BF16, 0x00000)
    OstA = sbt("OstA", [128, 512], BF16, 0x01000)
    OstB = sbt("OstB", [128, 512], BF16, 0x01400)
    wbuf = sbt("wbuf", [128, 2, 4096], F32, 0x04000)
    wscr = sbt("wscr", [128, 4096], F32, 0x24200)
    qbst = sbt("qbst", [128, 2, 4096], BF16, 0x2E800)
    xn_f = sbt("xn_f", [128, IC, T], BF16, 0x04000)
    Qr = sbt("Qr", [128, IC, T], BF16, 0x0C000)
    Kr = sbt("Kr", [128, IC, T], BF16, 0x14000)
    Vt = sbt("Vt", [128, NTT, 1040], BF16, 0x1C000)
    xdb4 = sbt("xdb4", [128, 2, 4, D], F32, 0x14000)
    tscr4 = sbt("tscr4", [128, 4 * D], F32, 0x0C000)
    xnf4 = sbt("xnf4", [128, 4, D], F32, 0x10000)
    xnbf4 = sbt("xnbf4", [128, 2, 4, D], BF16, 0x24200)
    xqbf4 = sbt("xqbf4", [128, 2, 4, D], BF16, 0x00000)
    qstage = sbt("qstage", [128, 2, 512], BF16, 0x24200)
    rs1 = sbt("rs1", [128, 512], BF16, 0x24A00)
    rs2 = sbt("rs2", [128, 512], BF16, 0x24E00)
    wgq = sbt("wgq", [128, IC, INNER], FP8, 0x00000)
    wuq = sbt("wuq", [128, IC, INNER], FP8, 0x08000)
    xqf3 = sbt("xqf3", [128, IC, T], BF16, 0x10000)
    h2 = sbt("h2", [128, 2, INNER], F32, 0x18000)
    hqst = sbt("hqst", [128, 2, INNER], BF16, 0x20000)
    sgus = sbt("sgus", [128, 4, 512], F32, 0x24000)
    hqfb = sbt("hqfb", [128, 2, T], BF16, 0x26000)
    wdq = sbt("wdq", [128, KC, D], BF16, 0x00000)
    hqt = sbt("hqt", [128, 2, KC, 128], BF16, 0x10000)
    Ore = sbt("Ore", [128, 2, 1040], BF16, 0x14000)
    x3 = sbt("x3", [128, 2, D], F32, 0x15060)
    ffn = sbt("ffn", [128, D], F32, 0x17060)
    attn = sbt("attn", [128, D], F32, 0x18060)
    sum1 = sbt("sum1", [128, D], F32, 0x19060)
    outsb = sbt("outsb", [128, 2, D], F32, 0x1A060)
    rl16 = sbt("rl16", [128, 16], F32, 0x1C060)
    cc = sbt("ccs", [128, 16], F32, 0x2A000)
    perm = sbt("perms", [128, 128], BF16, 0x2A240)
    ropeC = sbt("ropeCs", [128, T], BF16, 0x2A340)
    ropeD = sbt("ropeDs", [128, T], BF16, 0x2B340)
    ones_bf = sbt("ones_bf", [128, 1], BF16, 0x2C340)
    ones_f = sbt("ones_f", [128, 1], F32, 0x2C360)
    onesr_f = sbt("onesr_f", [1, 128], F32, 0x2C380)
    stats = sbt("stats", [128, 64], F32, 0x2C580)
    rsx_cols = sbt("rsx_cols", [128, 32], F32, 0x2C680)
    sc_cols = sbt("sc_cols", [128, 32], F32, 0x2C700)
    mw_rep = sbt("mw_rep", [128, 4], F32, 0x2C780)
    lst = sbt("lst", [128, 2, T], BF16, 0x2C7A0)

    pbig = [nc.alloc_psum_tensor("pbig%d" % i, [128, 1024], F32).ap() for i in range(2)]
    pb = [nc.alloc_psum_tensor("pbk%d" % i, [128, 512], F32).ap() for i in range(4)]

    import contextlib
    es = contextlib.ExitStack()
    sems = {}
    for name in ("in", "g", "st", "pe", "act", "dve", "pool"):
        sems[name] = es.enter_context(nc.semaphore("s_" + name))
    S = Sched(nc, sems)
    C = S.cnt

    def col(j):
        return stats[:, j : j + 1]

    def ccol(j):
        return cc[:, j : j + 1]

    S.dma("sync", "in", cc[:], cc_d)
    S.dma("sync", "in", perm[:], perm_d)
    S.dma("sync", "in", ropeC[:], ropeC_d)
    S.dma("sync", "in", ropeD[:], ropeD_d)
    S.put("gpsimd", _mset, ones_bf[:], 1.0)
    S.put("gpsimd", _mset, ones_f[:], 1.0)
    S.put("gpsimd", _mset, onesr_f[:], 1.0)
    S.put("gpsimd", _mset, Vt[:], 1.0, inc="pool")
    S.wait("sync", "in", C["in"])
    S.flush()

    def wchunk(wd_ap, mode, ch):
        if mode == "ic":
            return wd_ap[ch * 128 : (ch + 1) * 128, :]
        return wd_ap[ch * 512 : (ch + 1) * 512, :].rearrange("(c k) m -> k c m", c=4)

    def wview(sb_ap, mode):
        if mode == "ic":
            return sb_ap
        return sb_ap.rearrange("p (c m) -> p c m", c=4)

    pw_state = {}

    def emit_pw_pass1(w_ap, mode):
        rd = [0] * 8
        for ch in range(8):
            if ch >= 2:
                S.wait("sync", "dve", rd[ch - 2])
            cl = S.dma("sync", "in", wview(wbuf[:, ch % 2, :], mode), wchunk(w_ap, mode, ch))
            S.wait("vector", "in", cl)
            rd[ch] = S.put("vector", _red, col(32 + ch), wbuf[:, ch % 2, :], ALU.add, True, inc="dve")
        pw_state["rd7"] = rd[7]

    def emit_pw_rest(w_ap, wq_ap, wi, mode):
        S.wait("vector", "dve", pw_state["rd7"])
        c1 = S.put("vector", _red, col(40), stats[:, 32:40], ALU.add, False, inc="dve")
        S.wait("tensor", "dve", c1)
        c2 = S.put("tensor", _mm, pb[2][0:1, 0:1], ones_f[:], col(40), True, True, None, inc="pe")
        S.wait("scalar", "pe", c2)
        c3 = S.put("scalar", _cp, stats[0:1, 41:42], pb[2][0:1, 0:1], inc="act")
        S.wait("tensor", "act", c3)
        c4 = S.put("tensor", _mm, pb[3][:, 0:1], onesr_f[:], stats[0:1, 41:42], True, True, None, inc="pe")
        S.wait("scalar", "pe", c4)
        c5 = S.put("scalar", _cp, col(42), pb[3][:, 0:1], inc="act")
        S.wait("vector", "act", c5)
        e = S.put("vector", _ts, col(43), col(42), 1.0 / NW, None, ALU.mult, None, inc="dve")
        S.wait("vector", "dve", e)
        e = S.put("vector", _ts, mw_rep[:, wi : wi + 1], col(43), 1e-5, None, ALU.max, None, inc="dve")
        S.wait("vector", "dve", e)
        e = S.put("vector", _recip, col(44), mw_rep[:, wi : wi + 1], inc="dve")
        S.wait("vector", "dve", e)
        q3 = [0] * 8
        sp = [0] * 8
        for ch in range(8):
            if ch >= 2:
                S.wait("sync", "dve", q3[ch - 2])
            cl = S.dma("sync", "in", wview(wbuf[:, ch % 2, :], mode), wchunk(w_ap, mode, ch))
            S.wait("vector", "in", cl)
            e = S.put("vector", _ts, wscr[:], wbuf[:, ch % 2, :], col(44), MAGIC, ALU.mult, ALU.add, inc="dve")
            S.wait("vector", "dve", e)
            e = S.put("vector", _ts, wscr[:], wscr[:], -MAGIC, 1.0, ALU.add, ALU.min, inc="dve")
            S.wait("vector", "dve", e)
            if ch >= 2:
                S.wait("vector", "g", sp[ch - 2])
            q3[ch] = S.put("vector", _ts, qbst[:, ch % 2, :], wscr[:], -1.0, None, ALU.max, None, inc="dve")
            S.wait("gpsimd", "dve", q3[ch])
            sp[ch] = S.dma("gpsimd", "g", wchunk(wq_ap, mode, ch), wview(qbst[:, ch % 2, :], mode))
        pw_state["spills"] = sp[7]

    pw_items = [(wgT_d, wgq_s, 0, "ic"), (wuT_d, wuq_s, 1, "ic"), (wdT_d, wdq_s, 2, "g4")]


    c_xn_spill = [0, 0]
    c_xq_spill = [0, 0]

    for s in range(2):
        NG = 4
        xn4_act = [0] * NG
        q4_dve = [0] * NG
        c_sp_xn = [0] * NG
        c_sp_xq = [0] * NG
        for g in range(NG):
            gpar = g % 2
            rows = s * T + g * 512
            if g >= 2:
                S.wait("sync", "act", xn4_act[g - 2])
            cl = S.dma("sync", "in", xdb4[:, gpar, :, :],
                       x_d[rows : rows + 512, :].rearrange("(t p) d -> p t d", p=128))
            xg = xdb4[:, gpar, :, :]
            S.wait("vector", "in", cl)
            r1 = S.put("vector", _red, stats[:, 0:4], xg, ALU.add, False, inc="dve")
            S.wait("scalar", "in", cl)
            if g >= 1:
                S.wait("scalar", "dve", q4_dve[g - 1])
            r2 = S.put("scalar", _act, tscr4[:], xg.rearrange("p t d -> p (t d)"), AF.Square, ccol(11), ccol(10), inc="act")
            S.wait("vector", "act", r2)
            e = S.put("vector", _red, stats[:, 4:8], tscr4.rearrange("p (t d) -> p t d", t=4), ALU.add, False, inc="dve")
            S.wait("vector", "dve", e)
            e = S.put("vector", _ts, stats[:, 8:12], stats[:, 0:4], 1.0 / D, None, ALU.mult, None, inc="dve")
            S.wait("vector", "dve", e)
            e = S.put("vector", _ts, stats[:, 12:16], stats[:, 4:8], 1.0 / D, None, ALU.mult, None, inc="dve")
            S.wait("vector", "dve", e)
            e = S.put("vector", _tt, stats[:, 16:20], stats[:, 8:12], stats[:, 8:12], ALU.mult, inc="dve")
            S.wait("vector", "dve", e)
            e = S.put("vector", _tt, stats[:, 20:24], stats[:, 12:16], stats[:, 16:20], ALU.subtract, inc="dve")
            S.wait("vector", "dve", e)
            e = S.put("vector", _ts, stats[:, 24:28], stats[:, 20:24], 1e-5, None, ALU.add, None, inc="dve")
            S.wait("scalar", "dve", e)
            a = S.put("scalar", _act, stats[:, 28:32], stats[:, 24:28], AF.Ln, ccol(11), ccol(10), inc="act")
            S.wait("scalar", "act", a)
            a = S.put("scalar", _act, stats[:, 32:36], stats[:, 28:32], AF.Exp, ccol(11), ccol(8), inc="act")
            S.wait("vector", "act", a)
            e = S.put("vector", _tt, stats[:, 36:40], stats[:, 32:36], stats[:, 32:36], ALU.mult, inc="dve")
            S.wait("vector", "dve", e)
            e = S.put("vector", _tt, stats[:, 40:44], stats[:, 36:40], stats[:, 24:28], ALU.mult, inc="dve")
            S.wait("vector", "dve", e)
            e = S.put("vector", _ts, stats[:, 44:48], stats[:, 40:44], -0.5, 1.5, ALU.mult, ALU.add, inc="dve")
            S.wait("vector", "dve", e)
            e = S.put("vector", _tt, stats[:, 48:52], stats[:, 32:36], stats[:, 44:48], ALU.mult, inc="dve")
            S.wait("vector", "dve", e)
            eb = S.put("vector", _stt, stats[:, 52:56], stats[:, 8:12], -1.0, stats[:, 48:52], ALU.mult, ALU.mult, inc="dve")
            S.wait("scalar", "dve", eb)
            for t in range(4):
                a = S.put("scalar", _act, xnf4[:, t, :], xg[:, t, :], AF.Identity,
                          stats[:, 52 + t : 53 + t], stats[:, 48 + t : 49 + t], inc="act")
            S.wait("scalar", "act", a)
            if g >= 2:
                S.wait("scalar", "g", c_sp_xn[g - 2])
            xn4_act[g] = S.put("scalar", _cp, xnbf4[:, gpar, :, :].rearrange("p t d -> p (t d)"), xnf4.rearrange("p t d -> p (t d)"), inc="act")
            S.wait("vector", "act", xn4_act[g])
            e = S.put("vector", _red, stats[:, 56:60], xnf4[:], ALU.max, True, inc="dve")
            S.wait("vector", "dve", e)
            e = S.put("vector", _ts, stats[:, 60:64], stats[:, 56:60], 1e-5, None, ALU.max, None, inc="dve")
            S.wait("vector", "dve", e)
            e = S.put("vector", _recip, stats[:, 0:4], stats[:, 60:64], inc="dve")
            S.wait("vector", "dve", e)
            e = S.put("vector", _ts, stats[:, 4:8], stats[:, 0:4], 127.0, None, ALU.mult, None, inc="dve")
            S.wait("vector", "dve", e)
            e = S.put("vector", _ts, rsx_cols[:, s * 16 + 4 * g : s * 16 + 4 * g + 4], stats[:, 60:64], 1.0 / 127.0, None, ALU.mult, None, inc="dve")
            S.wait("vector", "dve", e)
            for t in range(4):
                e = S.put("vector", _ts, xnf4[:, t, :], xnf4[:, t, :], stats[:, 4 + t : 5 + t], MAGIC, ALU.mult, ALU.add, inc="dve")
            S.wait("vector", "dve", e)
            if g >= 2:
                S.wait("vector", "g", c_sp_xq[g - 2])
            q4_dve[g] = S.put("vector", _ts, xqbf4[:, gpar, :, :].rearrange("p t d -> p (t d)"), xnf4.rearrange("p t d -> p (t d)"), -MAGIC, None, ALU.add, None, inc="dve")
            S.wait("gpsimd", "act", xn4_act[g])
            c_sp_xn[g] = S.dma("gpsimd", "g", xn_s[g * 512 : (g + 1) * 512, :].rearrange("(t p) d -> p t d", p=128), xnbf4[:, gpar, :, :])
            S.wait("gpsimd", "dve", q4_dve[g])
            c_sp_xq[g] = S.dma("gpsimd", "g", xq_s[rows : rows + 512, :].rearrange("(t p) d -> p t d", p=128), xqbf4[:, gpar, :, :])
        c_xn_spill[s] = c_sp_xn[NG - 1]
        c_xq_spill[s] = c_sp_xq[NG - 1]
        S.flush()

        S.wait("sync", "g", c_xn_spill[s])
        for ic in range(IC):
            S.dma("sync", "in", xn_f[:, ic, :], xn_s[:, ic * 128 : (ic + 1) * 128], transpose=True)
        c_xnf = C["in"]

        ev_bank = [0, 0]
        sw_bank = [0, 0]
        qst_use = [0, 0]
        prev_rot = 0
        c_proj_mm = 0
        for (wsrc, dstR) in ((wqT_d, Qr), (wkT_d, Kr)):
            S.wait("gpsimd", "pe", c_proj_mm)
            cw = S.dma("gpsimd", "g", wbufQKV[:], wsrc.rearrange("(c k) m -> k c m", c=IC))
            for oc in range(IC):
                for tc in range(NQC):
                    idx = oc * NQC + tc
                    par = idx % 2
                    tsl = slice(tc * 512, tc * 512 + 512)
                    if idx == 0:
                        S.wait("tensor", "in", c_xnf)
                        S.wait("tensor", "g", cw)
                    S.wait("tensor", "act", ev_bank[par])
                    cmm = 0
                    for ic in range(IC):
                        incv = "pe" if ic == IC - 1 else None
                        r = S.put("tensor", _mm, pb[par][:], wbufQKV[:, ic, oc * 128 : (oc + 1) * 128], xn_f[:, ic, tsl], ic == 0, ic == IC - 1, None, inc=incv)
                        if incv:
                            cmm = r
                    c_proj_mm = cmm
                    S.wait("scalar", "pe", cmm)
                    S.wait("scalar", "dve", qst_use[par])
                    ev_bank[par] = S.put("scalar", _cp, qstage[:, par, :], pb[par][:], inc="act")
                    S.wait("tensor", "act", ev_bank[par])
                    S.wait("tensor", "dve", sw_bank[par])
                    csw = S.put("tensor", _mm, pb[2 + par][:], perm[:], qstage[:, par, :], True, True, None, inc="pe")
                    S.wait("vector", "pe", csw)
                    S.wait("vector", "dve", prev_rot)
                    sw_bank[par] = S.put("vector", _tt, rs2[:], pb[2 + par][:], ropeD[:, tsl], ALU.mult, inc="dve")
                    ct1 = S.put("vector", _tt, rs1[:], qstage[:, par, :], ropeC[:, tsl], ALU.mult, inc="dve")
                    S.wait("vector", "dve", ct1)
                    prev_rot = S.put("vector", _tt, dstR[:, oc, tsl], rs1[:], rs2[:], ALU.add, inc="dve")
                    qst_use[par] = prev_rot
        S.wait("gpsimd", "pe", c_proj_mm)
        cwv = S.dma("gpsimd", "g", wbufQKV[:], wvT_d.rearrange("(c k) m -> k c m", c=IC))
        vev = [0, 0]
        for tt in range(NTT):
            for vc in range(2):
                idx = tt * 2 + vc
                par = idx % 2
                if idx == 0:
                    S.wait("tensor", "g", cwv)
                S.wait("tensor", "act", vev[par])
                cmm = 0
                for ic in range(IC):
                    incv = "pe" if ic == IC - 1 else None
                    r = S.put("tensor", _mm, pbig[0][:, par * 512 : par * 512 + 512], xn_f[:, ic, tt * 128 : (tt + 1) * 128], wbufQKV[:, ic, vc * 512 : (vc + 1) * 512], ic == 0, ic == IC - 1, None, inc=incv)
                    if incv:
                        cmm = r
                S.wait("scalar", "pe", cmm)
                vdst = Vt[:, tt, :].rearrange("p (h e) -> p h e", e=65)[:, vc * 8 : (vc + 1) * 8, 0:64]
                vsrc = pbig[0][:, par * 512 : par * 512 + 512].rearrange("p (h d) -> p h d", d=64)
                vev[par] = S.put("scalar", _cp, vdst, vsrc, inc="act")
        S.flush()

        c_osp = 0
        c_lsp = 0
        c_oev = 0
        c_lev = 0
        for hp in range(NHP):
            if s == 0:
                if hp in (0, 2, 4):
                    emit_pw_pass1(*[pw_items[hp // 2][0], pw_items[hp // 2][3]])
                elif hp in (1, 3, 5):
                    it = pw_items[hp // 2]
                    emit_pw_rest(it[0], it[1], it[2], it[3])
            for qc in range(NQC):
                qsl = slice(qc * 512, qc * 512 + 512)
                pe0 = C["pe"]
                act0 = C["act"]
                c_smm = [pe0 + (1 if u == 0 else 2 * u) for u in range(NU)]
                c_av = [pe0 + (2 * u + 3 if u < NU - 1 else 2 * NU) for u in range(NU)]
                c_exp = [act0 + u + 1 for u in range(NU)]

                def emit_S(u):
                    par = u % 2
                    usl = slice(u * 128, u * 128 + 128)
                    if u >= 2:
                        S.wait("tensor", "act", c_exp[u - 2])
                    S.put("tensor", _mm, pbig[par][:, 0:512], Kr[0:64, hp, usl], Qr[0:64, hp, qsl], True, True, (0, 0))
                    r = S.put("tensor", _mm, pbig[par][:, 512:1024], Kr[64:128, hp, usl], Qr[64:128, hp, qsl], True, True, (64, 0), inc="pe")
                    assert r == c_smm[u], (r, c_smm[u])

                def emit_AV(u):
                    par = u % 2
                    st_ = u == 0
                    sp_ = u == NU - 1
                    S.wait("tensor", "act", c_exp[u])
                    if u == 0:
                        S.wait("tensor", "act", c_oev)
                        S.wait("tensor", "act", c_lev)
                    S.put("tensor", _mm, pb[0][0:65, :], Vt[:, u, 130 * hp : 130 * hp + 65], Psb[:, par, 0:512], st_, sp_, (0, 0))
                    r = S.put("tensor", _mm, pb[1][0:65, :], Vt[:, u, 130 * hp + 65 : 130 * hp + 130], Psb[:, par, 512:1024], st_, sp_, (0, 0), inc="pe")
                    assert r == c_av[u], (r, c_av[u])

                emit_S(0)
                for u in range(1, NU):
                    emit_S(u)
                    emit_AV(u - 1)
                emit_AV(NU - 1)

                for u in range(NU):
                    par = u % 2
                    S.wait("scalar", "pe", c_smm[u])
                    if u >= 2:
                        S.wait("scalar", "pe", c_av[u - 2])
                    r = S.put("scalar", _act, Psb[:, par, :], pbig[par][:], AF.Exp, ccol(2), ccol(3), inc="act")
                    assert r == c_exp[u], (r, c_exp[u])
                S.wait("scalar", "pe", c_av[NU - 1])
                S.wait("scalar", "g", c_osp)
                S.put("scalar", _cp, OstA[0:64, :], pb[0][0:64, :])
                c_oev = S.put("scalar", _cp, OstB[0:64, :], pb[1][0:64, :], inc="act")
                S.wait("scalar", "g", c_lsp)
                S.put("scalar", _cp, lst[64:65, 0, qsl], pb[0][64:65, :])
                c_lev = S.put("scalar", _cp, lst[64:65, 1, qsl], pb[1][64:65, :], inc="act")
                S.wait("gpsimd", "act", c_oev)
                S.dma("gpsimd", "g", O_s[s * 1040 + 128 * hp : s * 1040 + 128 * hp + 64, qsl], OstA[0:64, :])
                c_osp = S.dma("gpsimd", "g", O_s[s * 1040 + 128 * hp + 64 : s * 1040 + 128 * hp + 128, qsl], OstB[0:64, :])
            S.wait("gpsimd", "act", c_lev)
            c_lsp = S.dma("gpsimd", "g", O_s[s * 1040 + 1024 + 2 * hp : s * 1040 + 1024 + 2 * hp + 2, :], lst[64:65, :, :])
        S.flush()

    S.wait("sync", "g", pw_state["spills"])
    S.dma("sync", "in", wgq[:], wgq_s.rearrange("(c k) m -> k c m", c=IC))
    S.dma("sync", "in", wuq[:], wuq_s.rearrange("(c k) m -> k c m", c=IC))
    c_wld = C["in"]

    c_lastmm = 0
    c_hq_spill = [0, 0]
    c_rt2 = [0, 0]
    c_sil = [0] * IC
    c_us = [0] * IC
    c_h = [0] * IC
    prev_sil_last = 0
    rt0_loads = [0] * KC
    rt0_done = 0
    for s in range(2):
        S.wait("sync", "g", c_xq_spill[s])
        if s == 1:
            S.wait("sync", "pe", c_lastmm)
        for ic in range(IC):
            S.dma("sync", "in", xqf3[:, ic, :], xq_s[s * T : (s + 1) * T, ic * 128 : (ic + 1) * 128], transpose=True)
        c_xqf = C["in"]
        c_hsp = [0] * NTT
        for tt in range(NTT):
            gtt = s * NTT + tt
            hpar = tt % 2
            S.wait("vector", "act", prev_sil_last)
            c_gx = S.put("vector", _tt, col(48), rsx_cols[:, gtt : gtt + 1], mw_rep[:, 0:1], ALU.mult, inc="dve")
            c_ux = S.put("vector", _tt, col(52), rsx_cols[:, gtt : gtt + 1], mw_rep[:, 1:2], ALU.mult, inc="dve")
            if s == 1:
                for kk in (2 * tt, 2 * tt + 1):
                    S.wait("sync", "g", c_hq_spill[0])
                    if kk >= 2:
                        S.wait("sync", "g", rt0_resp[kk - 2])
                    rt0_loads[kk] = S.dma("sync", "in", hqfb[:, kk % 2, :], hq_s[0:T, kk * 128 : (kk + 1) * 128], transpose=True)
                    if kk == 0:
                        rt0_resp = [0] * KC
                    S.wait("gpsimd", "in", rt0_loads[kk])
                    rt0_resp[kk] = S.dma("gpsimd", "g", hq_s2[kk, :, 0:T], hqfb[:, kk % 2, :])
                if tt == NTT - 1:
                    c_rt2[0] = rt0_resp[KC - 1]
            for og in range(IC):
                opar = og % 2
                osl = slice(og * 512, og * 512 + 512)
                if tt == 0 and og == 0:
                    S.wait("tensor", "in", max(c_wld, c_xqf))
                pidx = og - 2 if og >= 2 else IC - 2 + og
                S.wait("tensor", "act", c_us[pidx])
                cg = cu = 0
                for ic in range(IC):
                    incv = "pe" if ic == IC - 1 else None
                    r = S.put("tensor", _mm, pb[opar][:], xqf3[:, ic, tt * 128 : (tt + 1) * 128], wgq[:, ic, osl], ic == 0, ic == IC - 1, None, inc=incv)
                    if incv:
                        cg = r
                for ic in range(IC):
                    incv = "pe" if ic == IC - 1 else None
                    r = S.put("tensor", _mm, pb[2 + opar][:], xqf3[:, ic, tt * 128 : (tt + 1) * 128], wuq[:, ic, osl], ic == 0, ic == IC - 1, None, inc=incv)
                    if incv:
                        cu = r
                c_lastmm = cu
                S.wait("scalar", "pe", cg)
                S.wait("scalar", "dve", c_gx)
                S.wait("scalar", "pool", c_h[pidx])
                c_sil[og] = S.put("scalar", _act, sgus[:, opar, :], pb[opar][:], AF.Silu, ccol(11), col(48), inc="act")
                S.wait("scalar", "pe", cu)
                S.wait("scalar", "dve", c_ux)
                c_us[og] = S.put("scalar", _act, sgus[:, 2 + opar, :], pb[2 + opar][:], AF.Identity, ccol(11), col(52), inc="act")
                S.wait("gpsimd", "act", c_us[og])
                c_h[og] = S.put("gpsimd", _tt, h2[:, hpar, osl], sgus[:, opar, :], sgus[:, 2 + opar, :], ALU.mult, inc="pool")
            prev_sil_last = c_us[IC - 1]
            S.wait("vector", "pool", c_h[IC - 1])
            e = S.put("vector", _red, col(56), h2[:, hpar, :], ALU.max, True, inc="dve")
            S.wait("vector", "dve", e)
            e = S.put("vector", _ts, col(57), col(56), 1e-5, None, ALU.max, None, inc="dve")
            S.wait("vector", "dve", e)
            e = S.put("vector", _recip, col(58), col(57), inc="dve")
            S.wait("vector", "dve", e)
            e = S.put("vector", _ts, col(59), col(58), 127.0, None, ALU.mult, None, inc="dve")
            S.wait("vector", "dve", e)
            e = S.put("vector", _ts, sc_cols[:, gtt : gtt + 1], col(57), mw_rep[:, 2:3], 1.0 / 127.0, ALU.mult, ALU.mult, inc="dve")
            S.wait("vector", "dve", e)
            e = S.put("vector", _ts, h2[:, hpar, :], h2[:, hpar, :], col(59), MAGIC, ALU.mult, ALU.add, inc="dve")
            S.wait("vector", "dve", e)
            if tt >= 2:
                S.wait("vector", "g", c_hsp[tt - 2])
            chq = S.put("vector", _ts, hqst[:, hpar, :], h2[:, hpar, :], -MAGIC, None, ALU.add, None, inc="dve")
            S.wait("gpsimd", "dve", chq)
            c_hsp[tt] = S.dma("gpsimd", "g", hq_s[gtt * 128 : gtt * 128 + 128, :], hqst[:, hpar, :])
        c_hq_spill[s] = c_hsp[NTT - 1]
    S.wait("sync", "pe", c_lastmm)
    S.dma("sync", "in", wdq[:], wdq_s.rearrange("(c k) m -> k c m", c=KC))
    c_wdq = C["in"]
    S.flush()

    hq_r = hq_s2.rearrange("k p t -> p k t")
    rt1_loads = [0] * KC
    rt1_resp = [0] * KC
    c_dmm = [0] * 32
    c_ffn = [0] * 32
    c_sum = [0] * 32
    c_at = [0] * 32
    for gtt in range(32):
        s2 = gtt // NTT
        tt = gtt % NTT
        par = gtt % 2
        if gtt < NTT:
            for kk in (2 * gtt, 2 * gtt + 1):
                S.wait("sync", "g", c_hq_spill[1])
                if kk >= 2:
                    S.wait("sync", "g", rt1_resp[kk - 2])
                rt1_loads[kk] = S.dma("sync", "in", hqfb[:, kk % 2, :], hq_s[T : 2 * T, kk * 128 : (kk + 1) * 128], transpose=True)
                S.wait("gpsimd", "in", rt1_loads[kk])
                rt1_resp[kk] = S.dma("gpsimd", "g", hq_s2[kk, :, T : 2 * T], hqfb[:, kk % 2, :])
                if kk == KC - 1:
                    c_rt2[1] = rt1_resp[kk]
        S.wait("sync", "g", c_rt2[s2])
        if gtt >= 2:
            S.wait("sync", "pe", c_dmm[gtt - 2])
            S.wait("sync", "pool", c_sum[gtt - 2])
        S.dma("sync", "in", hqt[:, par, :, :], hq_r[:, :, gtt * 128 : gtt * 128 + 128])
        S.dma("sync", "in", x3[:, par, :], x_d[gtt * 128 : gtt * 128 + 128, :])
        cin = S.dma("sync", "in", Ore[:, par, :], O_s[s2 * 1040 : (s2 + 1) * 1040, tt * 128 : (tt + 1) * 128], transpose=True)
        S.wait("tensor", "in", max(cin, c_wdq))
        if gtt >= 2:
            S.wait("tensor", "act", c_ffn[gtt - 2])
        for dc in range(2):
            dsl = slice(dc * 512, dc * 512 + 512)
            for kc in range(KC):
                incv = "pe" if (kc == KC - 1 and dc == 1) else None
                r = S.put("tensor", _mm, pb[dc * 2 + par][:], hqt[:, par, kc, :], wdq[:, kc, dsl], kc == 0, kc == KC - 1, None, inc=incv)
                if incv:
                    c_dmm[gtt] = r
        S.wait("scalar", "pe", c_dmm[gtt])
        if gtt >= 1:
            S.wait("scalar", "pool", c_sum[gtt - 1])
        S.put("scalar", _act, ffn[:, 0:512], pb[par][:], AF.Identity, ccol(11), sc_cols[:, gtt : gtt + 1])
        c_ffn[gtt] = S.put("scalar", _act, ffn[:, 512:1024], pb[2 + par][:], AF.Identity, ccol(11), sc_cols[:, gtt : gtt + 1], inc="act")
        S.wait("vector", "in", cin)
        if gtt >= 1:
            S.wait("vector", "pool", c_sum[gtt - 1])
            S.wait("vector", "dve", c_at[gtt - 1])
        e = S.put("vector", _recip, rl16[:], Ore[:, par, 1024:1040], inc="dve")
        S.wait("vector", "dve", e)
        c_at[gtt] = S.put(
            "vector", _tt,
            attn.rearrange("p (h d) -> p h d", h=16),
            Ore[:, par, 0:1024].rearrange("p (h d) -> p h d", h=16),
            rl16.rearrange("p (h o) -> p h o", o=1).broadcast_to([128, 16, 64]),
            ALU.mult,
            inc="dve",
        )
        S.wait("gpsimd", "dve", c_at[gtt])
        S.wait("gpsimd", "act", c_ffn[gtt])
        e = S.put("gpsimd", _tt, sum1[:], attn[:], ffn[:], ALU.add, inc="pool")
        S.wait("gpsimd", "pool", e)
        c_sum[gtt] = S.put("gpsimd", _tt, outsb[:, par, :], sum1[:], x3[:, par, :], ALU.add, inc="pool")
        S.wait("gpsimd", "pool", c_sum[gtt])
        S.dma("gpsimd", "st", out_d[gtt * 128 : gtt * 128 + 128, :], outsb[:, par, :])
    S.wait("sync", "st", C["st"])
    S.flush()
    es.close()
    return nc


_NC_CACHE = {}


def _get_nc():
    if "nc" not in _NC_CACHE:
        _NC_CACHE["nc"] = build_program()
    return _NC_CACHE["nc"]


def _host_consts():
    cols = np.zeros((128, 16), dtype=np.float32)
    vals = [MAGIC, -MAGIC, EXPB, 0.125, 1.0 / 1024.0, 1e-5, 127.0, 1.0 / 127.0,
            -0.5, 1.5, 1.0, 0.0]
    for j, v in enumerate(vals):
        cols[:, j] = v
    permm = np.zeros((128, 128), dtype=np.float32)
    for k in range(128):
        permm[k, k ^ 1] = 1.0
    inv = 10000.0 ** (-np.arange(0, 64, 2, dtype=np.float64) / 64.0)
    t = np.arange(T, dtype=np.float64)
    f = np.outer(t, inv)
    cosf = np.cos(f)
    sinf = np.sin(f)
    Cm = np.zeros((128, T), dtype=np.float64)
    Dm = np.zeros((128, T), dtype=np.float64)
    for p in range(128):
        d = p % 64
        j = d // 2
        Cm[p] = cosf[:, j]
        Dm[p] = sinf[:, j] if (d % 2 == 1) else -sinf[:, j]
    bf = ml_dtypes.bfloat16
    return (cols, permm.astype(bf), Cm.astype(np.float32).astype(bf),
            Dm.astype(np.float32).astype(bf))


def kernel(L0_attn_x, L0_attn_wq, L0_attn_wk, L0_attn_wv,
           L0_ffn_w_gate, L0_ffn_w_up, L0_ffn_w_down):
    x = np.ascontiguousarray(np.asarray(L0_attn_x, dtype=np.float32))
    wqT = np.ascontiguousarray(np.asarray(L0_attn_wq, dtype=np.float32).T)
    wkT = np.ascontiguousarray(np.asarray(L0_attn_wk, dtype=np.float32).T)
    wvT = np.ascontiguousarray(np.asarray(L0_attn_wv, dtype=np.float32).T)
    wgT = np.ascontiguousarray(np.asarray(L0_ffn_w_gate, dtype=np.float32).T)
    wuT = np.ascontiguousarray(np.asarray(L0_ffn_w_up, dtype=np.float32).T)
    wdT = np.ascontiguousarray(np.asarray(L0_ffn_w_down, dtype=np.float32).T)
    cols, permm, Cm, Dm = _host_consts()

    nc = _get_nc()
    in_maps = []
    for c in range(8):
        shard = np.ascontiguousarray(x[2 * c : 2 * c + 2].reshape(2 * T, D))
        in_maps.append({
            "x": shard, "wqT": wqT, "wkT": wkT, "wvT": wvT,
            "wgT": wgT, "wuT": wuT, "wdT": wdT,
            "cc": cols, "perm": permm, "ropeC": Cm, "ropeD": Dm,
        })
    res = run_bass_kernel_spmd(nc, in_maps, core_ids=list(range(8)))
    out = np.empty((16, T, D), dtype=np.float32)
    for c in range(8):
        out[2 * c : 2 * c + 2] = np.asarray(res.results[c]["out"]).reshape(2, T, D)
    return out


# version 11
# speedup vs baseline: 1.0850x; 1.0850x over previous
import numpy as np
import ml_dtypes
import concourse.bass as bass
import concourse.mybir as mybir
from concourse.bass_utils import run_bass_kernel_spmd

F32 = mybir.dt.float32
BF16 = mybir.dt.bfloat16
FP8 = mybir.dt.float8e4
AF = mybir.ActivationFunctionType
ALU = mybir.AluOpType
AX = mybir.AxisListType

MAGIC = 1.5 * 2.0 ** 23
EXPB = -30.0
T = 2048
D = 1024
IC = 8
INNER = 4096
KC = 32
NTT = 16
NQC = 4
NU = 16
NHP = 8
NW = float(INNER * D)

ENGS = ("sync", "tensor", "scalar", "vector", "gpsimd")


class Sched:
    def __init__(self, nc, sems):
        self.nc = nc
        self.sems = sems
        self.cnt = {k: 0 for k in sems}
        self.q = {e: [] for e in ENGS}

    def put(self, eng, fn, *args, inc=None, n=1):
        if inc is None:
            self.q[eng].append((fn, args, None, 0))
            return None
        self.q[eng].append((fn, args, self.sems[inc], n))
        self.cnt[inc] += n
        return self.cnt[inc]

    def wait(self, eng, key, val):
        if val is None or val <= 0:
            return
        sem = self.sems[key]

        def w(e, _s=sem, _v=val):
            e.wait_ge(_s, _v)

        self.q[eng].append((w, (), None, 0))

    def dma(self, eng, key, out, in_, transpose=False):
        if transpose:
            def fn(e, o, i):
                return e.dma_start_transpose(o, i)
        else:
            def fn(e, o, i):
                return e.dma_start(out=o, in_=i)
        return self.put(eng, fn, out, in_, inc=key, n=16)

    def flush(self):
        qs, self.q = self.q, {e: [] for e in ENGS}
        with self.nc.Block() as blk:
            for eng in ENGS:
                if not qs[eng]:
                    continue

                def run(e, _lst=qs[eng]):
                    for fn, args, sem, n in _lst:
                        ins = fn(e, *args)
                        if sem is not None:
                            ins.then_inc(sem, n)

                getattr(blk, eng)(run)


def _mm(e, out, l, r, st, sp, tp):
    return e.matmul(out, l, r, start=st, stop=sp, tile_position=tp)


def _act(e, out, in_, f, b, s):
    if b is None and s is None:
        return e.activation(out, in_, f)
    if b is None:
        return e.activation(out, in_, f, scale=s)
    if s is None:
        return e.activation(out, in_, f, bias=b)
    return e.activation(out, in_, f, bias=b, scale=s)


def _cp(e, o, i):
    return e.copy(o, i)


def _tt(e, o, a, b, op):
    return e.tensor_tensor(o, a, b, op=op)


def _ts(e, o, a, s1, s2, o0, o1):
    if s2 is None:
        return e.tensor_single_scalar(o, a, s1, op=o0)
    return e.tensor_scalar(o, a, s1, s2, op0=o0, op1=o1)


def _stt(e, o, a, sc, b, o0, o1):
    return e.scalar_tensor_tensor(o, a, sc, b, op0=o0, op1=o1)


def _red(e, o, i, op, absv):
    return e.tensor_reduce(o, i, axis=AX.X, op=op, apply_absolute_value=absv)


def _recip(e, o, i):
    return e.reciprocal(o, i)


def _mset(e, ap, v):
    return e.memset(ap, v)


def build_program():
    nc = bass.Bass("TRN2", target_bir_lowering=False)

    x_d = nc.dram_tensor("x", [2 * T, D], F32, kind="ExternalInput").ap()
    wqT_d = nc.dram_tensor("wqT", [D, D], F32, kind="ExternalInput").ap()
    wkT_d = nc.dram_tensor("wkT", [D, D], F32, kind="ExternalInput").ap()
    wvT_d = nc.dram_tensor("wvT", [D, D], F32, kind="ExternalInput").ap()
    wgT_d = nc.dram_tensor("wgT", [D, INNER], F32, kind="ExternalInput").ap()
    wuT_d = nc.dram_tensor("wuT", [D, INNER], F32, kind="ExternalInput").ap()
    wdT_d = nc.dram_tensor("wdT", [INNER, D], F32, kind="ExternalInput").ap()
    cc_d = nc.dram_tensor("cc", [128, 16], F32, kind="ExternalInput").ap()
    perm_d = nc.dram_tensor("perm", [128, 128], BF16, kind="ExternalInput").ap()
    ropeC_d = nc.dram_tensor("ropeC", [128, T], BF16, kind="ExternalInput").ap()
    ropeD_d = nc.dram_tensor("ropeD", [128, T], BF16, kind="ExternalInput").ap()
    out_d = nc.dram_tensor("out", [2 * T, D], F32, kind="ExternalOutput").ap()

    xn_s = nc.dram_tensor("xn_s", [T, D], BF16, kind="Internal").ap()
    xq_s = nc.dram_tensor("xq_s", [2 * T, D], BF16, kind="Internal").ap()
    O_s = nc.dram_tensor("O_s", [2 * 1040, T], BF16, kind="Internal").ap()
    wgq_s = nc.dram_tensor("wgq_s", [D, INNER], FP8, kind="Internal").ap()
    wuq_s = nc.dram_tensor("wuq_s", [D, INNER], FP8, kind="Internal").ap()
    wdq_s = nc.dram_tensor("wdq_s", [INNER, D], BF16, kind="Internal").ap()
    hq_s = nc.dram_tensor("hq_s", [2 * T, INNER], BF16, kind="Internal").ap()
    hq_s2 = nc.dram_tensor("hq_s2", [KC, 128, 2 * T], BF16, kind="Internal").ap()

    SBUF_BASE = 0x4000

    def sbt(name, shape, dt, off):
        return nc.alloc_sbuf_tensor_at(name, shape, dt, offset=SBUF_BASE + off).ap()

    wbufQKV = sbt("wbufQKV", [128, IC, D], BF16, 0x00000)
    Psb = sbt("Psb", [128, 2, 1024], BF16, 0x00000)
    OstA = sbt("OstA", [128, 512], BF16, 0x01000)
    OstB = sbt("OstB", [128, 512], BF16, 0x01400)
    wbuf = sbt("wbuf", [128, 2, 4096], F32, 0x04000)
    wscr = sbt("wscr", [128, 4096], F32, 0x24200)
    qbst = sbt("qbst", [128, 2, 4096], BF16, 0x2E800)
    xn_f = sbt("xn_f", [128, IC, T], BF16, 0x04000)
    Qr = sbt("Qr", [128, IC, T], BF16, 0x0C000)
    Kr = sbt("Kr", [128, IC, T], BF16, 0x14000)
    Vt = sbt("Vt", [128, NTT, 1040], BF16, 0x1C000)
    xdb4 = sbt("xdb4", [128, 2, 4, D], F32, 0x14000)
    tscr4 = sbt("tscr4", [128, 4 * D], F32, 0x0C000)
    xnf4 = sbt("xnf4", [128, 4, D], F32, 0x10000)
    xnbf4 = sbt("xnbf4", [128, 2, 4, D], BF16, 0x24200)
    xqbf4 = sbt("xqbf4", [128, 2, 4, D], BF16, 0x00000)
    qstage = sbt("qstage", [128, 2, 512], BF16, 0x24200)
    rs1 = sbt("rs1", [128, 512], BF16, 0x24A00)
    rs2 = sbt("rs2", [128, 512], BF16, 0x24E00)
    wgq = sbt("wgq", [128, IC, INNER], FP8, 0x00000)
    wuq = sbt("wuq", [128, IC, INNER], FP8, 0x08000)
    xqf3 = sbt("xqf3", [128, IC, T], BF16, 0x10000)
    h2 = sbt("h2", [128, 2, INNER], F32, 0x18000)
    hqst = sbt("hqst", [128, 1, INNER], BF16, 0x20000)
    sgus = sbt("sgus", [128, 8, 512], F32, 0x22000)
    hqfb = sbt("hqfb", [128, 4, T], BF16, 0x26000)
    wdq = sbt("wdq", [128, KC, D], BF16, 0x00000)
    hqt = sbt("hqt", [128, 3, KC, 128], BF16, 0x10000)
    Ore = sbt("Ore", [128, 3, 1040], BF16, 0x16000)
    x3 = sbt("x3", [128, 3, D], F32, 0x17900)
    ffn = sbt("ffn", [128, D], F32, 0x1A900)
    attn = sbt("attn", [128, D], F32, 0x1B900)
    sum1 = sbt("sum1", [128, D], F32, 0x1C900)
    outsb = sbt("outsb", [128, 2, D], F32, 0x1D900)
    rl16 = sbt("rl16", [128, 16], F32, 0x1F900)
    cc = sbt("ccs", [128, 16], F32, 0x2A000)
    perm = sbt("perms", [128, 128], BF16, 0x2A240)
    ropeC = sbt("ropeCs", [128, T], BF16, 0x2A340)
    ropeD = sbt("ropeDs", [128, T], BF16, 0x2B340)
    ones_bf = sbt("ones_bf", [128, 1], BF16, 0x2C340)
    ones_f = sbt("ones_f", [128, 1], F32, 0x2C360)
    onesr_f = sbt("onesr_f", [1, 128], F32, 0x2C380)
    stats = sbt("stats", [128, 64], F32, 0x2C580)
    rsx_cols = sbt("rsx_cols", [128, 32], F32, 0x2C680)
    sc_cols = sbt("sc_cols", [128, 32], F32, 0x2C700)
    mw_rep = sbt("mw_rep", [128, 4], F32, 0x2C780)
    lst = sbt("lst", [128, 2, T], BF16, 0x2C7A0)

    pbig = [nc.alloc_psum_tensor("pbig%d" % i, [128, 1024], F32).ap() for i in range(2)]
    pb = [nc.alloc_psum_tensor("pbk%d" % i, [128, 512], F32).ap() for i in range(4)]

    import contextlib
    es = contextlib.ExitStack()
    sems = {}
    for name in ("in", "g", "st", "pe", "act", "dve", "pool"):
        sems[name] = es.enter_context(nc.semaphore("s_" + name))
    S = Sched(nc, sems)
    C = S.cnt

    def col(j):
        return stats[:, j : j + 1]

    def ccol(j):
        return cc[:, j : j + 1]

    S.dma("sync", "in", cc[:], cc_d)
    S.dma("sync", "in", perm[:], perm_d)
    S.dma("sync", "in", ropeC[:], ropeC_d)
    S.dma("sync", "in", ropeD[:], ropeD_d)
    S.put("gpsimd", _mset, ones_bf[:], 1.0)
    S.put("gpsimd", _mset, ones_f[:], 1.0)
    S.put("gpsimd", _mset, onesr_f[:], 1.0)
    S.put("gpsimd", _mset, Vt[:], 1.0, inc="pool")
    S.wait("sync", "in", C["in"])
    S.flush()

    def wchunk(wd_ap, mode, ch):
        if mode == "ic":
            return wd_ap[ch * 128 : (ch + 1) * 128, :]
        return wd_ap[ch * 512 : (ch + 1) * 512, :].rearrange("(c k) m -> k c m", c=4)

    def wview(sb_ap, mode):
        if mode == "ic":
            return sb_ap
        return sb_ap.rearrange("p (c m) -> p c m", c=4)

    pw_state = {}

    def emit_pw_pass1(w_ap, mode):
        rd = [0] * 8
        for ch in range(8):
            if ch >= 2:
                S.wait("sync", "dve", rd[ch - 2])
            cl = S.dma("sync", "in", wview(wbuf[:, ch % 2, :], mode), wchunk(w_ap, mode, ch))
            S.wait("vector", "in", cl)
            rd[ch] = S.put("vector", _red, col(32 + ch), wbuf[:, ch % 2, :], ALU.add, True, inc="dve")
        pw_state["rd7"] = rd[7]

    def emit_pw_rest(w_ap, wq_ap, wi, mode):
        S.wait("vector", "dve", pw_state["rd7"])
        c1 = S.put("vector", _red, col(40), stats[:, 32:40], ALU.add, False, inc="dve")
        S.wait("tensor", "dve", c1)
        c2 = S.put("tensor", _mm, pb[2][0:1, 0:1], ones_f[:], col(40), True, True, None, inc="pe")
        S.wait("scalar", "pe", c2)
        c3 = S.put("scalar", _cp, stats[0:1, 41:42], pb[2][0:1, 0:1], inc="act")
        S.wait("tensor", "act", c3)
        c4 = S.put("tensor", _mm, pb[3][:, 0:1], onesr_f[:], stats[0:1, 41:42], True, True, None, inc="pe")
        S.wait("scalar", "pe", c4)
        c5 = S.put("scalar", _cp, col(42), pb[3][:, 0:1], inc="act")
        S.wait("vector", "act", c5)
        e = S.put("vector", _ts, col(43), col(42), 1.0 / NW, None, ALU.mult, None, inc="dve")
        S.wait("vector", "dve", e)
        e = S.put("vector", _ts, mw_rep[:, wi : wi + 1], col(43), 1e-5, None, ALU.max, None, inc="dve")
        S.wait("vector", "dve", e)
        e = S.put("vector", _recip, col(44), mw_rep[:, wi : wi + 1], inc="dve")
        S.wait("vector", "dve", e)
        q3 = [0] * 8
        sp = [0] * 8
        for ch in range(8):
            if ch >= 2:
                S.wait("sync", "dve", q3[ch - 2])
            cl = S.dma("sync", "in", wview(wbuf[:, ch % 2, :], mode), wchunk(w_ap, mode, ch))
            S.wait("vector", "in", cl)
            e = S.put("vector", _ts, wscr[:], wbuf[:, ch % 2, :], col(44), MAGIC, ALU.mult, ALU.add, inc="dve")
            S.wait("vector", "dve", e)
            e = S.put("vector", _ts, wscr[:], wscr[:], -MAGIC, 1.0, ALU.add, ALU.min, inc="dve")
            S.wait("vector", "dve", e)
            if ch >= 2:
                S.wait("vector", "g", sp[ch - 2])
            q3[ch] = S.put("vector", _ts, qbst[:, ch % 2, :], wscr[:], -1.0, None, ALU.max, None, inc="dve")
            S.wait("gpsimd", "dve", q3[ch])
            sp[ch] = S.dma("gpsimd", "g", wchunk(wq_ap, mode, ch), wview(qbst[:, ch % 2, :], mode))
        pw_state["spills"] = sp[7]

    pw_items = [(wgT_d, wgq_s, 0, "ic"), (wuT_d, wuq_s, 1, "ic"), (wdT_d, wdq_s, 2, "g4")]


    c_xn_spill = [0, 0]
    c_xq_spill = [0, 0]

    for s in range(2):
        NG = 4
        xn4_act = [0] * NG
        q4_dve = [0] * NG
        c_sp_xn = [0] * NG
        c_sp_xq = [0] * NG
        for g in range(NG):
            gpar = g % 2
            rows = s * T + g * 512
            if g >= 2:
                S.wait("sync", "act", xn4_act[g - 2])
            cl = S.dma("sync", "in", xdb4[:, gpar, :, :],
                       x_d[rows : rows + 512, :].rearrange("(t p) d -> p t d", p=128))
            xg = xdb4[:, gpar, :, :]
            S.wait("vector", "in", cl)
            r1 = S.put("vector", _red, stats[:, 0:4], xg, ALU.add, False, inc="dve")
            S.wait("scalar", "in", cl)
            if g >= 1:
                S.wait("scalar", "dve", q4_dve[g - 1])
            r2 = S.put("scalar", _act, tscr4[:], xg.rearrange("p t d -> p (t d)"), AF.Square, ccol(11), ccol(10), inc="act")
            S.wait("vector", "act", r2)
            e = S.put("vector", _red, stats[:, 4:8], tscr4.rearrange("p (t d) -> p t d", t=4), ALU.add, False, inc="dve")
            S.wait("vector", "dve", e)
            e = S.put("vector", _ts, stats[:, 8:12], stats[:, 0:4], 1.0 / D, None, ALU.mult, None, inc="dve")
            S.wait("vector", "dve", e)
            e = S.put("vector", _ts, stats[:, 12:16], stats[:, 4:8], 1.0 / D, None, ALU.mult, None, inc="dve")
            S.wait("vector", "dve", e)
            e = S.put("vector", _tt, stats[:, 16:20], stats[:, 8:12], stats[:, 8:12], ALU.mult, inc="dve")
            S.wait("vector", "dve", e)
            e = S.put("vector", _tt, stats[:, 20:24], stats[:, 12:16], stats[:, 16:20], ALU.subtract, inc="dve")
            S.wait("vector", "dve", e)
            e = S.put("vector", _ts, stats[:, 24:28], stats[:, 20:24], 1e-5, None, ALU.add, None, inc="dve")
            S.wait("scalar", "dve", e)
            a = S.put("scalar", _act, stats[:, 28:32], stats[:, 24:28], AF.Ln, ccol(11), ccol(10), inc="act")
            S.wait("scalar", "act", a)
            a = S.put("scalar", _act, stats[:, 32:36], stats[:, 28:32], AF.Exp, ccol(11), ccol(8), inc="act")
            S.wait("vector", "act", a)
            e = S.put("vector", _tt, stats[:, 36:40], stats[:, 32:36], stats[:, 32:36], ALU.mult, inc="dve")
            S.wait("vector", "dve", e)
            e = S.put("vector", _tt, stats[:, 40:44], stats[:, 36:40], stats[:, 24:28], ALU.mult, inc="dve")
            S.wait("vector", "dve", e)
            e = S.put("vector", _ts, stats[:, 44:48], stats[:, 40:44], -0.5, 1.5, ALU.mult, ALU.add, inc="dve")
            S.wait("vector", "dve", e)
            e = S.put("vector", _tt, stats[:, 48:52], stats[:, 32:36], stats[:, 44:48], ALU.mult, inc="dve")
            S.wait("vector", "dve", e)
            eb = S.put("vector", _stt, stats[:, 52:56], stats[:, 8:12], -1.0, stats[:, 48:52], ALU.mult, ALU.mult, inc="dve")
            S.wait("scalar", "dve", eb)
            for t in range(4):
                a = S.put("scalar", _act, xnf4[:, t, :], xg[:, t, :], AF.Identity,
                          stats[:, 52 + t : 53 + t], stats[:, 48 + t : 49 + t], inc="act")
            S.wait("scalar", "act", a)
            if g >= 2:
                S.wait("scalar", "g", c_sp_xn[g - 2])
            xn4_act[g] = S.put("scalar", _cp, xnbf4[:, gpar, :, :].rearrange("p t d -> p (t d)"), xnf4.rearrange("p t d -> p (t d)"), inc="act")
            S.wait("vector", "act", xn4_act[g])
            e = S.put("vector", _red, stats[:, 56:60], xnf4[:], ALU.max, True, inc="dve")
            S.wait("vector", "dve", e)
            e = S.put("vector", _ts, stats[:, 60:64], stats[:, 56:60], 1e-5, None, ALU.max, None, inc="dve")
            S.wait("vector", "dve", e)
            e = S.put("vector", _recip, stats[:, 0:4], stats[:, 60:64], inc="dve")
            S.wait("vector", "dve", e)
            e = S.put("vector", _ts, stats[:, 4:8], stats[:, 0:4], 127.0, None, ALU.mult, None, inc="dve")
            S.wait("vector", "dve", e)
            e = S.put("vector", _ts, rsx_cols[:, s * 16 + 4 * g : s * 16 + 4 * g + 4], stats[:, 60:64], 1.0 / 127.0, None, ALU.mult, None, inc="dve")
            S.wait("vector", "dve", e)
            for t in range(4):
                e = S.put("vector", _ts, xnf4[:, t, :], xnf4[:, t, :], stats[:, 4 + t : 5 + t], MAGIC, ALU.mult, ALU.add, inc="dve")
            S.wait("vector", "dve", e)
            if g >= 2:
                S.wait("vector", "g", c_sp_xq[g - 2])
            q4_dve[g] = S.put("vector", _ts, xqbf4[:, gpar, :, :].rearrange("p t d -> p (t d)"), xnf4.rearrange("p t d -> p (t d)"), -MAGIC, None, ALU.add, None, inc="dve")
            S.wait("gpsimd", "act", xn4_act[g])
            c_sp_xn[g] = S.dma("gpsimd", "g", xn_s[g * 512 : (g + 1) * 512, :].rearrange("(t p) d -> p t d", p=128), xnbf4[:, gpar, :, :])
            S.wait("gpsimd", "dve", q4_dve[g])
            c_sp_xq[g] = S.dma("gpsimd", "g", xq_s[rows : rows + 512, :].rearrange("(t p) d -> p t d", p=128), xqbf4[:, gpar, :, :])
        c_xn_spill[s] = c_sp_xn[NG - 1]
        c_xq_spill[s] = c_sp_xq[NG - 1]
        S.flush()

        S.wait("sync", "g", c_xn_spill[s])
        for ic in range(IC):
            S.dma("sync", "in", xn_f[:, ic, :], xn_s[:, ic * 128 : (ic + 1) * 128], transpose=True)
        c_xnf = C["in"]

        ev_bank = [0, 0]
        sw_bank = [0, 0]
        qst_use = [0, 0]
        prev_rot = 0
        c_proj_mm = 0
        for (wsrc, dstR) in ((wqT_d, Qr), (wkT_d, Kr)):
            S.wait("gpsimd", "pe", c_proj_mm)
            cw = S.dma("gpsimd", "g", wbufQKV[:], wsrc.rearrange("(c k) m -> k c m", c=IC))
            for oc in range(IC):
                for tc in range(NQC):
                    idx = oc * NQC + tc
                    par = idx % 2
                    tsl = slice(tc * 512, tc * 512 + 512)
                    if idx == 0:
                        S.wait("tensor", "in", c_xnf)
                        S.wait("tensor", "g", cw)
                    S.wait("tensor", "act", ev_bank[par])
                    cmm = 0
                    for ic in range(IC):
                        incv = "pe" if ic == IC - 1 else None
                        r = S.put("tensor", _mm, pb[par][:], wbufQKV[:, ic, oc * 128 : (oc + 1) * 128], xn_f[:, ic, tsl], ic == 0, ic == IC - 1, None, inc=incv)
                        if incv:
                            cmm = r
                    c_proj_mm = cmm
                    S.wait("scalar", "pe", cmm)
                    S.wait("scalar", "dve", qst_use[par])
                    ev_bank[par] = S.put("scalar", _cp, qstage[:, par, :], pb[par][:], inc="act")
                    S.wait("tensor", "act", ev_bank[par])
                    S.wait("tensor", "dve", sw_bank[par])
                    csw = S.put("tensor", _mm, pb[2 + par][:], perm[:], qstage[:, par, :], True, True, None, inc="pe")
                    S.wait("vector", "pe", csw)
                    S.wait("vector", "dve", prev_rot)
                    sw_bank[par] = S.put("vector", _tt, rs2[:], pb[2 + par][:], ropeD[:, tsl], ALU.mult, inc="dve")
                    ct1 = S.put("vector", _tt, rs1[:], qstage[:, par, :], ropeC[:, tsl], ALU.mult, inc="dve")
                    S.wait("vector", "dve", ct1)
                    prev_rot = S.put("vector", _tt, dstR[:, oc, tsl], rs1[:], rs2[:], ALU.add, inc="dve")
                    qst_use[par] = prev_rot
        S.wait("gpsimd", "pe", c_proj_mm)
        cwv = S.dma("gpsimd", "g", wbufQKV[:], wvT_d.rearrange("(c k) m -> k c m", c=IC))
        vev = [0, 0]
        for tt in range(NTT):
            for vc in range(2):
                idx = tt * 2 + vc
                par = idx % 2
                if idx == 0:
                    S.wait("tensor", "g", cwv)
                S.wait("tensor", "act", vev[par])
                cmm = 0
                for ic in range(IC):
                    incv = "pe" if ic == IC - 1 else None
                    r = S.put("tensor", _mm, pbig[0][:, par * 512 : par * 512 + 512], xn_f[:, ic, tt * 128 : (tt + 1) * 128], wbufQKV[:, ic, vc * 512 : (vc + 1) * 512], ic == 0, ic == IC - 1, None, inc=incv)
                    if incv:
                        cmm = r
                S.wait("scalar", "pe", cmm)
                vdst = Vt[:, tt, :].rearrange("p (h e) -> p h e", e=65)[:, vc * 8 : (vc + 1) * 8, 0:64]
                vsrc = pbig[0][:, par * 512 : par * 512 + 512].rearrange("p (h d) -> p h d", d=64)
                vev[par] = S.put("scalar", _cp, vdst, vsrc, inc="act")
        S.flush()

        c_osp = 0
        c_lsp = 0
        c_oev = 0
        c_lev = 0
        for hp in range(NHP):
            if s == 0:
                if hp in (0, 2, 4):
                    emit_pw_pass1(*[pw_items[hp // 2][0], pw_items[hp // 2][3]])
                elif hp in (1, 3, 5):
                    it = pw_items[hp // 2]
                    emit_pw_rest(it[0], it[1], it[2], it[3])
            for qc in range(NQC):
                qsl = slice(qc * 512, qc * 512 + 512)
                pe0 = C["pe"]
                act0 = C["act"]
                c_smm = [pe0 + (1 if u == 0 else 2 * u) for u in range(NU)]
                c_av = [pe0 + (2 * u + 3 if u < NU - 1 else 2 * NU) for u in range(NU)]
                c_exp = [act0 + u + 1 for u in range(NU)]

                def emit_S(u):
                    par = u % 2
                    usl = slice(u * 128, u * 128 + 128)
                    if u >= 2:
                        S.wait("tensor", "act", c_exp[u - 2])
                    S.put("tensor", _mm, pbig[par][:, 0:512], Kr[0:64, hp, usl], Qr[0:64, hp, qsl], True, True, (0, 0))
                    r = S.put("tensor", _mm, pbig[par][:, 512:1024], Kr[64:128, hp, usl], Qr[64:128, hp, qsl], True, True, (64, 0), inc="pe")
                    assert r == c_smm[u], (r, c_smm[u])

                def emit_AV(u):
                    par = u % 2
                    st_ = u == 0
                    sp_ = u == NU - 1
                    S.wait("tensor", "act", c_exp[u])
                    if u == 0:
                        S.wait("tensor", "dve", c_oev)
                        S.wait("tensor", "dve", c_lev)
                    S.put("tensor", _mm, pb[0][0:65, :], Vt[:, u, 130 * hp : 130 * hp + 65], Psb[:, par, 0:512], st_, sp_, (0, 0))
                    r = S.put("tensor", _mm, pb[1][0:65, :], Vt[:, u, 130 * hp + 65 : 130 * hp + 130], Psb[:, par, 512:1024], st_, sp_, (0, 0), inc="pe")
                    assert r == c_av[u], (r, c_av[u])

                emit_S(0)
                for u in range(1, NU):
                    emit_S(u)
                    emit_AV(u - 1)
                emit_AV(NU - 1)

                for u in range(NU):
                    par = u % 2
                    S.wait("scalar", "pe", c_smm[u])
                    if u >= 2:
                        S.wait("scalar", "pe", c_av[u - 2])
                    r = S.put("scalar", _act, Psb[:, par, :], pbig[par][:], AF.Exp, ccol(2), ccol(3), inc="act")
                    assert r == c_exp[u], (r, c_exp[u])
                S.wait("vector", "pe", c_av[NU - 1])
                S.wait("vector", "g", c_osp)
                S.put("vector", lambda en, o, i: en.tensor_copy(o, i), OstA[0:64, :], pb[0][0:64, :])
                c_oev = S.put("vector", lambda en, o, i: en.tensor_copy(o, i), OstB[0:64, :], pb[1][0:64, :], inc="dve")
                S.wait("vector", "g", c_lsp)
                S.put("vector", lambda en, o, i: en.tensor_copy(o, i), lst[64:65, 0, qsl], pb[0][64:65, :])
                c_lev = S.put("vector", lambda en, o, i: en.tensor_copy(o, i), lst[64:65, 1, qsl], pb[1][64:65, :], inc="dve")
                S.wait("gpsimd", "dve", c_oev)
                S.dma("gpsimd", "g", O_s[s * 1040 + 128 * hp : s * 1040 + 128 * hp + 64, qsl], OstA[0:64, :])
                c_osp = S.dma("gpsimd", "g", O_s[s * 1040 + 128 * hp + 64 : s * 1040 + 128 * hp + 128, qsl], OstB[0:64, :])
            S.wait("gpsimd", "dve", c_lev)
            c_lsp = S.dma("gpsimd", "g", O_s[s * 1040 + 1024 + 2 * hp : s * 1040 + 1024 + 2 * hp + 2, :], lst[64:65, :, :])
        S.flush()

    S.wait("sync", "g", pw_state["spills"])
    S.dma("sync", "in", wgq[:], wgq_s.rearrange("(c k) m -> k c m", c=IC))
    S.dma("sync", "in", wuq[:], wuq_s.rearrange("(c k) m -> k c m", c=IC))
    c_wld = C["in"]

    c_lastmm = 0
    c_hq_spill = [0, 0]
    c_rt2 = [0, 0]
    c_sil = [0] * IC
    c_us = [0] * IC
    c_h = [0] * IC
    prev_sil_last = 0
    rt0_loads = [0] * KC
    rt0_done = 0
    for s in range(2):
        S.wait("sync", "g", c_xq_spill[s])
        if s == 1:
            S.wait("sync", "pe", c_lastmm)
        for ic in range(IC):
            S.dma("sync", "in", xqf3[:, ic, :], xq_s[s * T : (s + 1) * T, ic * 128 : (ic + 1) * 128], transpose=True)
        c_xqf = C["in"]
        c_hsp = [0] * NTT
        for tt in range(NTT):
            gtt = s * NTT + tt
            hpar = tt % 2
            S.wait("vector", "act", prev_sil_last)
            c_gx = S.put("vector", _tt, col(48), rsx_cols[:, gtt : gtt + 1], mw_rep[:, 0:1], ALU.mult, inc="dve")
            c_ux = S.put("vector", _tt, col(52), rsx_cols[:, gtt : gtt + 1], mw_rep[:, 1:2], ALU.mult, inc="dve")
            if s == 1:
                for kk in (2 * tt, 2 * tt + 1):
                    S.wait("sync", "g", c_hq_spill[0])
                    if kk >= 2:
                        S.wait("sync", "g", rt0_resp[kk - 2])
                    rt0_loads[kk] = S.dma("sync", "in", hqfb[:, kk % 2, :], hq_s[0:T, kk * 128 : (kk + 1) * 128], transpose=True)
                    if kk == 0:
                        rt0_resp = [0] * KC
                    S.wait("gpsimd", "in", rt0_loads[kk])
                    rt0_resp[kk] = S.dma("gpsimd", "g", hq_s2[kk, :, 0:T], hqfb[:, kk % 2, :])
                if tt == NTT - 1:
                    c_rt2[0] = rt0_resp[KC - 1]
            for og in range(IC):
                opar = og % 4
                gbank = pb[og % 2][:] if og % 4 < 2 else pbig[0][:, (og % 2) * 512 : (og % 2) * 512 + 512]
                ubank = pb[2 + og % 2][:] if og % 4 < 2 else pbig[1][:, (og % 2) * 512 : (og % 2) * 512 + 512]
                osl = slice(og * 512, og * 512 + 512)
                if tt == 0 and og == 0:
                    S.wait("tensor", "in", max(c_wld, c_xqf))
                pidx = og - 4 if og >= 4 else IC - 4 + og
                S.wait("tensor", "act", c_us[pidx])
                cg = cu = 0
                for ic in range(IC):
                    incv = "pe" if ic == IC - 1 else None
                    r = S.put("tensor", _mm, gbank, xqf3[:, ic, tt * 128 : (tt + 1) * 128], wgq[:, ic, osl], ic == 0, ic == IC - 1, None, inc=incv)
                    if incv:
                        cg = r
                for ic in range(IC):
                    incv = "pe" if ic == IC - 1 else None
                    r = S.put("tensor", _mm, ubank, xqf3[:, ic, tt * 128 : (tt + 1) * 128], wuq[:, ic, osl], ic == 0, ic == IC - 1, None, inc=incv)
                    if incv:
                        cu = r
                c_lastmm = cu
                S.wait("scalar", "pe", cg)
                S.wait("scalar", "dve", c_gx)
                S.wait("scalar", "pool", c_h[pidx])
                c_sil[og] = S.put("scalar", _act, sgus[:, opar, :], gbank, AF.Silu, ccol(11), col(48), inc="act")
                S.wait("scalar", "pe", cu)
                S.wait("scalar", "dve", c_ux)
                c_us[og] = S.put("scalar", _act, sgus[:, 4 + opar, :], ubank, AF.Identity, ccol(11), col(52), inc="act")
                S.wait("gpsimd", "act", c_us[og])
                c_h[og] = S.put("gpsimd", _tt, h2[:, hpar, osl], sgus[:, opar, :], sgus[:, 4 + opar, :], ALU.mult, inc="pool")
            prev_sil_last = c_us[IC - 1]
            S.wait("vector", "pool", c_h[IC - 1])
            e = S.put("vector", _red, col(56), h2[:, hpar, :], ALU.max, True, inc="dve")
            S.wait("vector", "dve", e)
            e = S.put("vector", _ts, col(57), col(56), 1e-5, None, ALU.max, None, inc="dve")
            S.wait("vector", "dve", e)
            e = S.put("vector", _recip, col(58), col(57), inc="dve")
            S.wait("vector", "dve", e)
            e = S.put("vector", _ts, col(59), col(58), 127.0, None, ALU.mult, None, inc="dve")
            S.wait("vector", "dve", e)
            e = S.put("vector", _ts, sc_cols[:, gtt : gtt + 1], col(57), mw_rep[:, 2:3], 1.0 / 127.0, ALU.mult, ALU.mult, inc="dve")
            S.wait("vector", "dve", e)
            e = S.put("vector", _ts, h2[:, hpar, :], h2[:, hpar, :], col(59), MAGIC, ALU.mult, ALU.add, inc="dve")
            S.wait("vector", "dve", e)
            if tt >= 1:
                S.wait("vector", "g", c_hsp[tt - 1])
            chq = S.put("vector", _ts, hqst[:, 0, :], h2[:, hpar, :], -MAGIC, None, ALU.add, None, inc="dve")
            S.wait("gpsimd", "dve", chq)
            c_hsp[tt] = S.dma("gpsimd", "g", hq_s[gtt * 128 : gtt * 128 + 128, :], hqst[:, 0, :])
        c_hq_spill[s] = c_hsp[NTT - 1]
    S.wait("sync", "pe", c_lastmm)
    S.dma("sync", "in", wdq[:], wdq_s.rearrange("(c k) m -> k c m", c=KC))
    c_wdq = C["in"]
    S.flush()

    hq_r = hq_s2.rearrange("k p t -> p k t")
    rt1_loads = [0] * KC
    rt1_resp = [0] * KC
    c_dmm = [0] * 32
    c_ffn = [0] * 32
    c_sum = [0] * 32
    c_at = [0] * 32
    for gtt in range(32):
        s2 = gtt // NTT
        tt = gtt % NTT
        par = gtt % 2
        par3 = gtt % 3
        if gtt < NTT:
            for kk in (2 * gtt, 2 * gtt + 1):
                S.wait("sync", "g", c_hq_spill[1])
                if kk >= 4:
                    S.wait("sync", "g", rt1_resp[kk - 4])
                rt1_loads[kk] = S.dma("sync", "in", hqfb[:, kk % 4, :], hq_s[T : 2 * T, kk * 128 : (kk + 1) * 128], transpose=True)
                S.wait("gpsimd", "in", rt1_loads[kk])
                rt1_resp[kk] = S.dma("gpsimd", "g", hq_s2[kk, :, T : 2 * T], hqfb[:, kk % 4, :])
                if kk == KC - 1:
                    c_rt2[1] = rt1_resp[kk]
        S.wait("sync", "g", c_rt2[s2])
        if gtt >= 3:
            S.wait("sync", "pe", c_dmm[gtt - 3])
            S.wait("sync", "pool", c_sum[gtt - 3])
        S.dma("sync", "in", hqt[:, par3, :, :], hq_r[:, :, gtt * 128 : gtt * 128 + 128])
        S.dma("sync", "in", x3[:, par3, :], x_d[gtt * 128 : gtt * 128 + 128, :])
        cin = S.dma("sync", "in", Ore[:, par3, :], O_s[s2 * 1040 : (s2 + 1) * 1040, tt * 128 : (tt + 1) * 128], transpose=True)
        S.wait("tensor", "in", max(cin, c_wdq))
        if gtt >= 2:
            S.wait("tensor", "act", c_ffn[gtt - 2])
        for dc in range(2):
            dsl = slice(dc * 512, dc * 512 + 512)
            for kc in range(KC):
                incv = "pe" if (kc == KC - 1 and dc == 1) else None
                r = S.put("tensor", _mm, pb[dc * 2 + par][:], hqt[:, par3, kc, :], wdq[:, kc, dsl], kc == 0, kc == KC - 1, None, inc=incv)
                if incv:
                    c_dmm[gtt] = r
        S.wait("scalar", "pe", c_dmm[gtt])
        if gtt >= 1:
            S.wait("scalar", "pool", c_sum[gtt - 1])
        S.put("scalar", _act, ffn[:, 0:512], pb[par][:], AF.Identity, ccol(11), sc_cols[:, gtt : gtt + 1])
        c_ffn[gtt] = S.put("scalar", _act, ffn[:, 512:1024], pb[2 + par][:], AF.Identity, ccol(11), sc_cols[:, gtt : gtt + 1], inc="act")
        S.wait("vector", "in", cin)
        if gtt >= 1:
            S.wait("vector", "pool", c_sum[gtt - 1])
            S.wait("vector", "dve", c_at[gtt - 1])
        e = S.put("vector", _recip, rl16[:], Ore[:, par3, 1024:1040], inc="dve")
        S.wait("vector", "dve", e)
        c_at[gtt] = S.put(
            "vector", _tt,
            attn.rearrange("p (h d) -> p h d", h=16),
            Ore[:, par3, 0:1024].rearrange("p (h d) -> p h d", h=16),
            rl16.rearrange("p (h o) -> p h o", o=1).broadcast_to([128, 16, 64]),
            ALU.mult,
            inc="dve",
        )
        S.wait("gpsimd", "dve", c_at[gtt])
        S.wait("gpsimd", "act", c_ffn[gtt])
        e = S.put("gpsimd", _tt, sum1[:], attn[:], ffn[:], ALU.add, inc="pool")
        S.wait("gpsimd", "pool", e)
        c_sum[gtt] = S.put("gpsimd", _tt, outsb[:, par, :], sum1[:], x3[:, par3, :], ALU.add, inc="pool")
        S.wait("gpsimd", "pool", c_sum[gtt])
        S.dma("gpsimd", "st", out_d[gtt * 128 : gtt * 128 + 128, :], outsb[:, par, :])
    S.wait("sync", "st", C["st"])
    S.flush()
    es.close()
    return nc


_NC_CACHE = {}


def _get_nc():
    if "nc" not in _NC_CACHE:
        _NC_CACHE["nc"] = build_program()
    return _NC_CACHE["nc"]


def _host_consts():
    cols = np.zeros((128, 16), dtype=np.float32)
    vals = [MAGIC, -MAGIC, EXPB, 0.125, 1.0 / 1024.0, 1e-5, 127.0, 1.0 / 127.0,
            -0.5, 1.5, 1.0, 0.0]
    for j, v in enumerate(vals):
        cols[:, j] = v
    permm = np.zeros((128, 128), dtype=np.float32)
    for k in range(128):
        permm[k, k ^ 1] = 1.0
    inv = 10000.0 ** (-np.arange(0, 64, 2, dtype=np.float64) / 64.0)
    t = np.arange(T, dtype=np.float64)
    f = np.outer(t, inv)
    cosf = np.cos(f)
    sinf = np.sin(f)
    Cm = np.zeros((128, T), dtype=np.float64)
    Dm = np.zeros((128, T), dtype=np.float64)
    for p in range(128):
        d = p % 64
        j = d // 2
        Cm[p] = cosf[:, j]
        Dm[p] = sinf[:, j] if (d % 2 == 1) else -sinf[:, j]
    bf = ml_dtypes.bfloat16
    return (cols, permm.astype(bf), Cm.astype(np.float32).astype(bf),
            Dm.astype(np.float32).astype(bf))


def kernel(L0_attn_x, L0_attn_wq, L0_attn_wk, L0_attn_wv,
           L0_ffn_w_gate, L0_ffn_w_up, L0_ffn_w_down):
    x = np.ascontiguousarray(np.asarray(L0_attn_x, dtype=np.float32))
    wqT = np.ascontiguousarray(np.asarray(L0_attn_wq, dtype=np.float32).T)
    wkT = np.ascontiguousarray(np.asarray(L0_attn_wk, dtype=np.float32).T)
    wvT = np.ascontiguousarray(np.asarray(L0_attn_wv, dtype=np.float32).T)
    wgT = np.ascontiguousarray(np.asarray(L0_ffn_w_gate, dtype=np.float32).T)
    wuT = np.ascontiguousarray(np.asarray(L0_ffn_w_up, dtype=np.float32).T)
    wdT = np.ascontiguousarray(np.asarray(L0_ffn_w_down, dtype=np.float32).T)
    cols, permm, Cm, Dm = _host_consts()

    nc = _get_nc()
    in_maps = []
    for c in range(8):
        shard = np.ascontiguousarray(x[2 * c : 2 * c + 2].reshape(2 * T, D))
        in_maps.append({
            "x": shard, "wqT": wqT, "wkT": wkT, "wvT": wvT,
            "wgT": wgT, "wuT": wuT, "wdT": wdT,
            "cc": cols, "perm": permm, "ropeC": Cm, "ropeD": Dm,
        })
    res = run_bass_kernel_spmd(nc, in_maps, core_ids=list(range(8)))
    out = np.empty((16, T, D), dtype=np.float32)
    for c in range(8):
        out[2 * c : 2 * c + 2] = np.asarray(res.results[c]["out"]).reshape(2, T, D)
    return out


# version 12 (final)
# speedup vs baseline: 1.1146x; 1.0273x over previous
import numpy as np
import ml_dtypes
import concourse.bass as bass
import concourse.mybir as mybir
from concourse.bass_utils import run_bass_kernel_spmd

F32 = mybir.dt.float32
BF16 = mybir.dt.bfloat16
FP8 = mybir.dt.float8e4
AF = mybir.ActivationFunctionType
ALU = mybir.AluOpType
AX = mybir.AxisListType

MAGIC = 1.5 * 2.0 ** 23
EXPB = -30.0
T = 2048
D = 1024
IC = 8
INNER = 4096
KC = 32
NTT = 16
NQC = 4
NU = 16
NHP = 8
NW = float(INNER * D)

ENGS = ("sync", "tensor", "scalar", "vector", "gpsimd")


class Sched:
    def __init__(self, nc, sems):
        self.nc = nc
        self.sems = sems
        self.cnt = {k: 0 for k in sems}
        self.q = {e: [] for e in ENGS}

    def put(self, eng, fn, *args, inc=None, n=1):
        if inc is None:
            self.q[eng].append((fn, args, None, 0))
            return None
        self.q[eng].append((fn, args, self.sems[inc], n))
        self.cnt[inc] += n
        return self.cnt[inc]

    def wait(self, eng, key, val):
        if val is None or val <= 0:
            return
        sem = self.sems[key]

        def w(e, _s=sem, _v=val):
            e.wait_ge(_s, _v)

        self.q[eng].append((w, (), None, 0))

    def dma(self, eng, key, out, in_, transpose=False):
        if transpose:
            def fn(e, o, i):
                return e.dma_start_transpose(o, i)
        else:
            def fn(e, o, i):
                return e.dma_start(out=o, in_=i)
        return self.put(eng, fn, out, in_, inc=key, n=16)

    def flush(self):
        qs, self.q = self.q, {e: [] for e in ENGS}
        with self.nc.Block() as blk:
            for eng in ENGS:
                if not qs[eng]:
                    continue

                def run(e, _lst=qs[eng]):
                    for fn, args, sem, n in _lst:
                        ins = fn(e, *args)
                        if sem is not None:
                            ins.then_inc(sem, n)

                getattr(blk, eng)(run)


def _mm(e, out, l, r, st, sp, tp):
    return e.matmul(out, l, r, start=st, stop=sp, tile_position=tp)


def _act(e, out, in_, f, b, s):
    if b is None and s is None:
        return e.activation(out, in_, f)
    if b is None:
        return e.activation(out, in_, f, scale=s)
    if s is None:
        return e.activation(out, in_, f, bias=b)
    return e.activation(out, in_, f, bias=b, scale=s)


def _cp(e, o, i):
    return e.copy(o, i)


def _tt(e, o, a, b, op):
    return e.tensor_tensor(o, a, b, op=op)


def _ts(e, o, a, s1, s2, o0, o1):
    if s2 is None:
        return e.tensor_single_scalar(o, a, s1, op=o0)
    return e.tensor_scalar(o, a, s1, s2, op0=o0, op1=o1)


def _stt(e, o, a, sc, b, o0, o1):
    return e.scalar_tensor_tensor(o, a, sc, b, op0=o0, op1=o1)


def _red(e, o, i, op, absv):
    return e.tensor_reduce(o, i, axis=AX.X, op=op, apply_absolute_value=absv)


def _recip(e, o, i):
    return e.reciprocal(o, i)


def _mset(e, ap, v):
    return e.memset(ap, v)


def build_program():
    nc = bass.Bass("TRN2", target_bir_lowering=False)

    x_d = nc.dram_tensor("x", [2 * T, D], F32, kind="ExternalInput").ap()
    wqT_d = nc.dram_tensor("wqT", [D, D], F32, kind="ExternalInput").ap()
    wkT_d = nc.dram_tensor("wkT", [D, D], F32, kind="ExternalInput").ap()
    wvT_d = nc.dram_tensor("wvT", [D, D], F32, kind="ExternalInput").ap()
    wgT_d = nc.dram_tensor("wgT", [D, INNER], F32, kind="ExternalInput").ap()
    wuT_d = nc.dram_tensor("wuT", [D, INNER], F32, kind="ExternalInput").ap()
    wdT_d = nc.dram_tensor("wdT", [INNER, D], F32, kind="ExternalInput").ap()
    cc_d = nc.dram_tensor("cc", [128, 16], F32, kind="ExternalInput").ap()
    perm_d = nc.dram_tensor("perm", [128, 128], BF16, kind="ExternalInput").ap()
    ropeC_d = nc.dram_tensor("ropeC", [128, T], BF16, kind="ExternalInput").ap()
    ropeD_d = nc.dram_tensor("ropeD", [128, T], BF16, kind="ExternalInput").ap()
    out_d = nc.dram_tensor("out", [2 * T, D], F32, kind="ExternalOutput").ap()

    xn_s = nc.dram_tensor("xn_s", [T, D], BF16, kind="Internal").ap()
    xq_s = nc.dram_tensor("xq_s", [2 * T, D], BF16, kind="Internal").ap()
    O_s = nc.dram_tensor("O_s", [2 * 1040, T], BF16, kind="Internal").ap()
    wgq_s = nc.dram_tensor("wgq_s", [D, INNER], FP8, kind="Internal").ap()
    wuq_s = nc.dram_tensor("wuq_s", [D, INNER], FP8, kind="Internal").ap()
    wdq_s = nc.dram_tensor("wdq_s", [INNER, D], BF16, kind="Internal").ap()
    hq_s = nc.dram_tensor("hq_s", [2 * T, INNER], BF16, kind="Internal").ap()
    hq_s2 = nc.dram_tensor("hq_s2", [KC, 128, 2 * T], BF16, kind="Internal").ap()

    SBUF_BASE = 0x4000

    def sbt(name, shape, dt, off):
        return nc.alloc_sbuf_tensor_at(name, shape, dt, offset=SBUF_BASE + off).ap()

    wbufQKV = sbt("wbufQKV", [128, IC, D], BF16, 0x00000)
    Psb = sbt("Psb", [128, 2, 1024], BF16, 0x00000)
    OstA = sbt("OstA", [128, 512], BF16, 0x01000)
    OstB = sbt("OstB", [128, 512], BF16, 0x01400)
    wbuf = sbt("wbuf", [128, 2, 4096], F32, 0x04000)
    wscr = sbt("wscr", [128, 4096], F32, 0x24200)
    qbst = sbt("qbst", [128, 2, 4096], BF16, 0x2E800)
    xn_f = sbt("xn_f", [128, IC, T], BF16, 0x04000)
    Qr = sbt("Qr", [128, IC, T], BF16, 0x0C000)
    Kr = sbt("Kr", [128, IC, T], BF16, 0x14000)
    Vt = sbt("Vt", [128, NTT, 1040], BF16, 0x1C000)
    xqbf4 = sbt("xqbf4", [128, 2, 4, D], BF16, 0x00000)
    tscr4 = sbt("tscr4", [128, 2, 4 * D], F32, 0x04000)
    xnf4 = sbt("xnf4", [128, 2, 4, D], F32, 0x0C000)
    xdb4 = sbt("xdb4", [128, 2, 4, D], F32, 0x14000)
    qstage = sbt("qstage", [128, 2, 512], BF16, 0x24200)
    rs1 = sbt("rs1", [128, 512], BF16, 0x24A00)
    rs2 = sbt("rs2", [128, 512], BF16, 0x24E00)
    wgq = sbt("wgq", [128, IC, INNER], FP8, 0x00000)
    wuq = sbt("wuq", [128, IC, INNER], FP8, 0x08000)
    xqf3 = sbt("xqf3", [128, IC, T], BF16, 0x10000)
    h2 = sbt("h2", [128, 2, INNER], F32, 0x18000)
    hqst = sbt("hqst", [128, 1, INNER], BF16, 0x20000)
    sgus = sbt("sgus", [128, 8, 512], F32, 0x22000)
    hqfb = sbt("hqfb", [128, 4, T], BF16, 0x26000)
    wdq = sbt("wdq", [128, KC, D], BF16, 0x00000)
    hqt = sbt("hqt", [128, 3, KC, 128], BF16, 0x10000)
    Ore = sbt("Ore", [128, 3, 1040], BF16, 0x16000)
    x3 = sbt("x3", [128, 3, D], F32, 0x17900)
    ffn = sbt("ffn", [128, D], F32, 0x1A900)
    attn = sbt("attn", [128, D], F32, 0x1B900)
    sum1 = sbt("sum1", [128, D], F32, 0x1C900)
    outsb = sbt("outsb", [128, 2, D], F32, 0x1D900)
    rl16 = sbt("rl16", [128, 16], F32, 0x1F900)
    cc = sbt("ccs", [128, 16], F32, 0x2A000)
    perm = sbt("perms", [128, 128], BF16, 0x2A240)
    ropeC = sbt("ropeCs", [128, T], BF16, 0x2A340)
    ropeD = sbt("ropeDs", [128, T], BF16, 0x2B340)
    ones_bf = sbt("ones_bf", [128, 1], BF16, 0x2C340)
    ones_f = sbt("ones_f", [128, 1], F32, 0x2C360)
    onesr_f = sbt("onesr_f", [1, 128], F32, 0x2C380)
    stats = sbt("stats", [128, 64], F32, 0x2C580)
    rsx_cols = sbt("rsx_cols", [128, 32], F32, 0x2C680)
    sc_cols = sbt("sc_cols", [128, 32], F32, 0x2C700)
    mw_rep = sbt("mw_rep", [128, 4], F32, 0x2C780)
    lst = sbt("lst", [128, 2, T], BF16, 0x2C7A0)

    pbig = [nc.alloc_psum_tensor("pbig%d" % i, [128, 1024], F32).ap() for i in range(2)]
    pb = [nc.alloc_psum_tensor("pbk%d" % i, [128, 512], F32).ap() for i in range(4)]

    import contextlib
    es = contextlib.ExitStack()
    sems = {}
    for name in ("in", "g", "st", "pe", "act", "dve", "pool"):
        sems[name] = es.enter_context(nc.semaphore("s_" + name))
    S = Sched(nc, sems)
    C = S.cnt

    def col(j):
        return stats[:, j : j + 1]

    def ccol(j):
        return cc[:, j : j + 1]

    S.dma("sync", "in", cc[:], cc_d)
    S.dma("sync", "in", perm[:], perm_d)
    S.dma("sync", "in", ropeC[:], ropeC_d)
    S.dma("sync", "in", ropeD[:], ropeD_d)
    S.put("gpsimd", _mset, ones_bf[:], 1.0)
    S.put("gpsimd", _mset, ones_f[:], 1.0)
    S.put("gpsimd", _mset, onesr_f[:], 1.0)
    S.put("gpsimd", _mset, Vt[:], 1.0, inc="pool")
    S.wait("sync", "in", C["in"])
    S.flush()

    def wchunk(wd_ap, mode, ch):
        if mode == "ic":
            return wd_ap[ch * 128 : (ch + 1) * 128, :]
        return wd_ap[ch * 512 : (ch + 1) * 512, :].rearrange("(c k) m -> k c m", c=4)

    def wview(sb_ap, mode):
        if mode == "ic":
            return sb_ap
        return sb_ap.rearrange("p (c m) -> p c m", c=4)

    pw_state = {}

    def emit_pw_pass1(w_ap, mode):
        rd = [0] * 8
        for ch in range(8):
            if ch >= 2:
                S.wait("sync", "dve", rd[ch - 2])
            cl = S.dma("sync", "in", wview(wbuf[:, ch % 2, :], mode), wchunk(w_ap, mode, ch))
            S.wait("vector", "in", cl)
            rd[ch] = S.put("vector", _red, col(32 + ch), wbuf[:, ch % 2, :], ALU.add, True, inc="dve")
        pw_state["rd7"] = rd[7]

    def emit_pw_rest(w_ap, wq_ap, wi, mode):
        S.wait("vector", "dve", pw_state["rd7"])
        c1 = S.put("vector", _red, col(40), stats[:, 32:40], ALU.add, False, inc="dve")
        S.wait("tensor", "dve", c1)
        c2 = S.put("tensor", _mm, pb[2][0:1, 0:1], ones_f[:], col(40), True, True, None, inc="pe")
        S.wait("scalar", "pe", c2)
        c3 = S.put("scalar", _cp, stats[0:1, 41:42], pb[2][0:1, 0:1], inc="act")
        S.wait("tensor", "act", c3)
        c4 = S.put("tensor", _mm, pb[3][:, 0:1], onesr_f[:], stats[0:1, 41:42], True, True, None, inc="pe")
        S.wait("scalar", "pe", c4)
        c5 = S.put("scalar", _cp, col(42), pb[3][:, 0:1], inc="act")
        S.wait("vector", "act", c5)
        e = S.put("vector", _ts, col(43), col(42), 1.0 / NW, None, ALU.mult, None, inc="dve")
        S.wait("vector", "dve", e)
        e = S.put("vector", _ts, mw_rep[:, wi : wi + 1], col(43), 1e-5, None, ALU.max, None, inc="dve")
        S.wait("vector", "dve", e)
        e = S.put("vector", _recip, col(44), mw_rep[:, wi : wi + 1], inc="dve")
        S.wait("vector", "dve", e)
        q3 = [0] * 8
        sp = [0] * 8
        for ch in range(8):
            if ch >= 2:
                S.wait("sync", "dve", q3[ch - 2])
            cl = S.dma("sync", "in", wview(wbuf[:, ch % 2, :], mode), wchunk(w_ap, mode, ch))
            S.wait("vector", "in", cl)
            e = S.put("vector", _ts, wscr[:], wbuf[:, ch % 2, :], col(44), MAGIC, ALU.mult, ALU.add, inc="dve")
            S.wait("vector", "dve", e)
            e = S.put("vector", _ts, wscr[:], wscr[:], -MAGIC, 1.0, ALU.add, ALU.min, inc="dve")
            S.wait("vector", "dve", e)
            if ch >= 2:
                S.wait("vector", "g", sp[ch - 2])
            q3[ch] = S.put("vector", _ts, qbst[:, ch % 2, :], wscr[:], -1.0, None, ALU.max, None, inc="dve")
            S.wait("gpsimd", "dve", q3[ch])
            sp[ch] = S.dma("gpsimd", "g", wchunk(wq_ap, mode, ch), wview(qbst[:, ch % 2, :], mode))
        pw_state["spills"] = sp[7]

    pw_items = [(wgT_d, wgq_s, 0, "ic"), (wuT_d, wuq_s, 1, "ic"), (wdT_d, wdq_s, 2, "g4")]


    c_xn_spill = [0, 0]
    c_xq_spill = [0, 0]

    for s in range(2):
        NG = 4
        c_ld = [0] * NG
        c_sq = [0] * NG
        c_r3 = [0] * NG
        c_xn = [0] * NG
        c_q4 = [0] * NG
        c_spn = [0] * NG
        c_spq = [0] * NG

        def xg_of(g):
            return xdb4[:, g % 2, :, :]

        def emit_load(g):
            rows = s * T + g * 512
            if g >= 2:
                S.wait("sync", "act", c_xn[g - 2])
            c_ld[g] = S.dma("sync", "in", xdb4[:, g % 2, :, :],
                            x_d[rows : rows + 512, :].rearrange("(t p) d -> p t d", p=128))

        def emit_sq(g):
            S.wait("scalar", "in", c_ld[g])
            if g >= 1:
                S.wait("scalar", "dve", c_r3[g - 1])
            c_sq[g] = S.put("scalar", _act, tscr4[:, g % 2, :], xg_of(g).rearrange("p t d -> p (t d)"), AF.Square, ccol(11), ccol(10), inc="act")

        def emit_chainA(g):
            S.wait("vector", "in", c_ld[g])
            e = S.put("vector", _red, stats[:, 0:4], xg_of(g), ALU.add, False, inc="dve")
            S.wait("vector", "act", c_sq[g])
            c_r3[g] = S.put("vector", _red, stats[:, 4:8], tscr4[:, g % 2, :].rearrange("p (t d) -> p t d", t=4), ALU.add, False, inc="dve")
            S.wait("vector", "dve", c_r3[g])
            e = S.put("vector", _ts, stats[:, 8:12], stats[:, 0:4], 1.0 / D, None, ALU.mult, None, inc="dve")
            S.wait("vector", "dve", e)
            e = S.put("vector", _ts, stats[:, 12:16], stats[:, 4:8], 1.0 / D, None, ALU.mult, None, inc="dve")
            S.wait("vector", "dve", e)
            e = S.put("vector", _tt, stats[:, 16:20], stats[:, 8:12], stats[:, 8:12], ALU.mult, inc="dve")
            S.wait("vector", "dve", e)
            e = S.put("vector", _tt, stats[:, 20:24], stats[:, 12:16], stats[:, 16:20], ALU.subtract, inc="dve")
            S.wait("vector", "dve", e)
            e = S.put("vector", _ts, stats[:, 24:28], stats[:, 20:24], 1e-5, None, ALU.add, None, inc="dve")
            return e

        def emit_lnexp(g, evarp):
            S.wait("scalar", "dve", evarp)
            a = S.put("scalar", _act, stats[:, 28:32], stats[:, 24:28], AF.Ln, ccol(11), ccol(10), inc="act")
            S.wait("scalar", "act", a)
            return S.put("scalar", _act, stats[:, 32:36], stats[:, 28:32], AF.Exp, ccol(11), ccol(8), inc="act")

        def emit_chainB(g, cexp):
            S.wait("vector", "act", cexp)
            e = S.put("vector", _tt, stats[:, 36:40], stats[:, 32:36], stats[:, 32:36], ALU.mult, inc="dve")
            S.wait("vector", "dve", e)
            e = S.put("vector", _tt, stats[:, 40:44], stats[:, 36:40], stats[:, 24:28], ALU.mult, inc="dve")
            S.wait("vector", "dve", e)
            e = S.put("vector", _ts, stats[:, 44:48], stats[:, 40:44], -0.5, 1.5, ALU.mult, ALU.add, inc="dve")
            S.wait("vector", "dve", e)
            e = S.put("vector", _tt, stats[:, 48:52], stats[:, 32:36], stats[:, 44:48], ALU.mult, inc="dve")
            S.wait("vector", "dve", e)
            e = S.put("vector", _stt, stats[:, 52:56], stats[:, 8:12], -1.0, stats[:, 48:52], ALU.mult, ALU.mult, inc="dve")
            return e

        def emit_xn(g, ebias):
            S.wait("scalar", "dve", ebias)
            if g >= 2:
                S.wait("scalar", "dve", c_q4[g - 2])
                S.wait("scalar", "g", c_spn[g - 2])
            a = 0
            for t in range(4):
                a = S.put("scalar", _act, xnf4[:, g % 2, t, :], xg_of(g)[:, t, :], AF.Identity,
                          stats[:, 52 + t : 53 + t], stats[:, 48 + t : 49 + t], inc="act")
            c_xn[g] = a

        def emit_quant(g):
            S.wait("vector", "act", c_xn[g])
            e = S.put("vector", _red, stats[:, 56:60], xnf4[:, g % 2, :, :], ALU.max, True, inc="dve")
            S.wait("vector", "dve", e)
            e = S.put("vector", _ts, stats[:, 60:64], stats[:, 56:60], 1e-5, None, ALU.max, None, inc="dve")
            S.wait("vector", "dve", e)
            e = S.put("vector", _recip, stats[:, 0:4], stats[:, 60:64], inc="dve")
            S.wait("vector", "dve", e)
            e = S.put("vector", _ts, stats[:, 4:8], stats[:, 0:4], 127.0, None, ALU.mult, None, inc="dve")
            S.wait("vector", "dve", e)
            e = S.put("vector", _ts, rsx_cols[:, s * 16 + 4 * g : s * 16 + 4 * g + 4], stats[:, 60:64], 1.0 / 127.0, None, ALU.mult, None, inc="dve")
            S.wait("vector", "dve", e)
            S.wait("gpsimd", "act", c_xn[g])
            c_spn[g] = S.dma("gpsimd", "g", xn_s[g * 512 : (g + 1) * 512, :].rearrange("(t p) d -> p t d", p=128), xnf4[:, g % 2, :, :])
            S.wait("vector", "g", c_spn[g])
            for t in range(4):
                e = S.put("vector", _ts, xnf4[:, g % 2, t, :], xnf4[:, g % 2, t, :], stats[:, 4 + t : 5 + t], MAGIC, ALU.mult, ALU.add, inc="dve")
            S.wait("vector", "dve", e)
            if g >= 2:
                S.wait("vector", "g", c_spq[g - 2])
            c_q4[g] = S.put("vector", _ts, xqbf4[:, g % 2, :, :].rearrange("p t d -> p (t d)"), xnf4[:, g % 2, :, :].rearrange("p t d -> p (t d)"), -MAGIC, None, ALU.add, None, inc="dve")
            S.wait("gpsimd", "dve", c_q4[g])
            rows = s * T + g * 512
            c_spq[g] = S.dma("gpsimd", "g", xq_s[rows : rows + 512, :].rearrange("(t p) d -> p t d", p=128), xqbf4[:, g % 2, :, :])

        emit_load(0)
        emit_sq(0)
        for g in range(NG):
            if g + 1 < NG:
                emit_load(g + 1)
                emit_sq(g + 1)
            evarp = emit_chainA(g)
            cexp = emit_lnexp(g, evarp)
            if g >= 1:
                emit_quant(g - 1)
            ebias = emit_chainB(g, cexp)
            emit_xn(g, ebias)
        emit_quant(NG - 1)
        c_xn_spill[s] = c_spn[NG - 1]
        c_xq_spill[s] = c_spq[NG - 1]
        S.flush()

        S.wait("sync", "g", c_xn_spill[s])
        for ic in range(IC):
            S.dma("sync", "in", xn_f[:, ic, :], xn_s[:, ic * 128 : (ic + 1) * 128], transpose=True)
        c_xnf = C["in"]

        ev_bank = [0, 0]
        sw_bank = [0, 0]
        qst_use = [0, 0]
        prev_rot = 0
        c_proj_mm = 0
        for (wsrc, dstR) in ((wqT_d, Qr), (wkT_d, Kr)):
            S.wait("gpsimd", "pe", c_proj_mm)
            cw = S.dma("gpsimd", "g", wbufQKV[:], wsrc.rearrange("(c k) m -> k c m", c=IC))
            for oc in range(IC):
                for tc in range(NQC):
                    idx = oc * NQC + tc
                    par = idx % 2
                    tsl = slice(tc * 512, tc * 512 + 512)
                    if idx == 0:
                        S.wait("tensor", "in", c_xnf)
                        S.wait("tensor", "g", cw)
                    S.wait("tensor", "act", ev_bank[par])
                    cmm = 0
                    for ic in range(IC):
                        incv = "pe" if ic == IC - 1 else None
                        r = S.put("tensor", _mm, pb[par][:], wbufQKV[:, ic, oc * 128 : (oc + 1) * 128], xn_f[:, ic, tsl], ic == 0, ic == IC - 1, None, inc=incv)
                        if incv:
                            cmm = r
                    c_proj_mm = cmm
                    S.wait("scalar", "pe", cmm)
                    S.wait("scalar", "dve", qst_use[par])
                    ev_bank[par] = S.put("scalar", _cp, qstage[:, par, :], pb[par][:], inc="act")
                    S.wait("tensor", "act", ev_bank[par])
                    S.wait("tensor", "dve", sw_bank[par])
                    csw = S.put("tensor", _mm, pb[2 + par][:], perm[:], qstage[:, par, :], True, True, None, inc="pe")
                    S.wait("vector", "pe", csw)
                    S.wait("vector", "dve", prev_rot)
                    sw_bank[par] = S.put("vector", _tt, rs2[:], pb[2 + par][:], ropeD[:, tsl], ALU.mult, inc="dve")
                    ct1 = S.put("vector", _tt, rs1[:], qstage[:, par, :], ropeC[:, tsl], ALU.mult, inc="dve")
                    S.wait("vector", "dve", ct1)
                    prev_rot = S.put("vector", _tt, dstR[:, oc, tsl], rs1[:], rs2[:], ALU.add, inc="dve")
                    qst_use[par] = prev_rot
        S.wait("gpsimd", "pe", c_proj_mm)
        cwv = S.dma("gpsimd", "g", wbufQKV[:], wvT_d.rearrange("(c k) m -> k c m", c=IC))
        vev = [0, 0]
        for tt in range(NTT):
            for vc in range(2):
                idx = tt * 2 + vc
                par = idx % 2
                if idx == 0:
                    S.wait("tensor", "g", cwv)
                S.wait("tensor", "act", vev[par])
                cmm = 0
                for ic in range(IC):
                    incv = "pe" if ic == IC - 1 else None
                    r = S.put("tensor", _mm, pbig[0][:, par * 512 : par * 512 + 512], xn_f[:, ic, tt * 128 : (tt + 1) * 128], wbufQKV[:, ic, vc * 512 : (vc + 1) * 512], ic == 0, ic == IC - 1, None, inc=incv)
                    if incv:
                        cmm = r
                S.wait("scalar", "pe", cmm)
                vdst = Vt[:, tt, :].rearrange("p (h e) -> p h e", e=65)[:, vc * 8 : (vc + 1) * 8, 0:64]
                vsrc = pbig[0][:, par * 512 : par * 512 + 512].rearrange("p (h d) -> p h d", d=64)
                vev[par] = S.put("scalar", _cp, vdst, vsrc, inc="act")
        S.flush()

        c_osp = 0
        c_lsp = 0
        c_oev = 0
        c_lev = 0
        for hp in range(NHP):
            if s == 0:
                if hp in (0, 2, 4):
                    emit_pw_pass1(*[pw_items[hp // 2][0], pw_items[hp // 2][3]])
                elif hp in (1, 3, 5):
                    it = pw_items[hp // 2]
                    emit_pw_rest(it[0], it[1], it[2], it[3])
            for qc in range(NQC):
                qsl = slice(qc * 512, qc * 512 + 512)
                pe0 = C["pe"]
                act0 = C["act"]
                c_smm = [pe0 + (1 if u == 0 else 2 * u) for u in range(NU)]
                c_av = [pe0 + (2 * u + 3 if u < NU - 1 else 2 * NU) for u in range(NU)]
                c_exp = [act0 + u + 1 for u in range(NU)]

                def emit_S(u):
                    par = u % 2
                    usl = slice(u * 128, u * 128 + 128)
                    if u >= 2:
                        S.wait("tensor", "act", c_exp[u - 2])
                    S.put("tensor", _mm, pbig[par][:, 0:512], Kr[0:64, hp, usl], Qr[0:64, hp, qsl], True, True, (0, 0))
                    r = S.put("tensor", _mm, pbig[par][:, 512:1024], Kr[64:128, hp, usl], Qr[64:128, hp, qsl], True, True, (64, 0), inc="pe")
                    assert r == c_smm[u], (r, c_smm[u])

                def emit_AV(u):
                    par = u % 2
                    st_ = u == 0
                    sp_ = u == NU - 1
                    S.wait("tensor", "act", c_exp[u])
                    if u == 0:
                        S.wait("tensor", "act", c_oev)
                        S.wait("tensor", "act", c_lev)
                    S.put("tensor", _mm, pb[0][0:65, :], Vt[:, u, 130 * hp : 130 * hp + 65], Psb[:, par, 0:512], st_, sp_, (0, 0))
                    r = S.put("tensor", _mm, pb[1][0:65, :], Vt[:, u, 130 * hp + 65 : 130 * hp + 130], Psb[:, par, 512:1024], st_, sp_, (0, 0), inc="pe")
                    assert r == c_av[u], (r, c_av[u])

                emit_S(0)
                for u in range(1, NU):
                    emit_S(u)
                    emit_AV(u - 1)
                emit_AV(NU - 1)

                for u in range(NU):
                    par = u % 2
                    S.wait("scalar", "pe", c_smm[u])
                    if u >= 2:
                        S.wait("scalar", "pe", c_av[u - 2])
                    r = S.put("scalar", _act, Psb[:, par, :], pbig[par][:], AF.Exp, ccol(2), ccol(3), inc="act")
                    assert r == c_exp[u], (r, c_exp[u])
                S.wait("scalar", "pe", c_av[NU - 1])
                S.wait("scalar", "g", c_osp)
                S.put("scalar", _cp, OstA[0:64, :], pb[0][0:64, :])
                c_oev = S.put("scalar", _cp, OstB[0:64, :], pb[1][0:64, :], inc="act")
                S.wait("scalar", "g", c_lsp)
                S.put("scalar", _cp, lst[64:65, 0, qsl], pb[0][64:65, :])
                c_lev = S.put("scalar", _cp, lst[64:65, 1, qsl], pb[1][64:65, :], inc="act")
                S.wait("gpsimd", "act", c_oev)
                S.dma("gpsimd", "g", O_s[s * 1040 + 128 * hp : s * 1040 + 128 * hp + 64, qsl], OstA[0:64, :])
                c_osp = S.dma("gpsimd", "g", O_s[s * 1040 + 128 * hp + 64 : s * 1040 + 128 * hp + 128, qsl], OstB[0:64, :])
            S.wait("gpsimd", "act", c_lev)
            c_lsp = S.dma("gpsimd", "g", O_s[s * 1040 + 1024 + 2 * hp : s * 1040 + 1024 + 2 * hp + 2, :], lst[64:65, :, :])
        S.flush()

    S.wait("sync", "g", pw_state["spills"])
    S.dma("sync", "in", wgq[:], wgq_s.rearrange("(c k) m -> k c m", c=IC))
    S.dma("sync", "in", wuq[:], wuq_s.rearrange("(c k) m -> k c m", c=IC))
    c_wld = C["in"]

    c_lastmm = 0
    c_hq_spill = [0, 0]
    c_rt2 = [0, 0]
    c_sil = [0] * IC
    c_us = [0] * IC
    c_h = [0] * IC
    prev_sil_last = 0
    rt0_loads = [0] * KC
    rt0_done = 0
    for s in range(2):
        S.wait("sync", "g", c_xq_spill[s])
        if s == 1:
            S.wait("sync", "pe", c_lastmm)
        for ic in range(IC):
            S.dma("sync", "in", xqf3[:, ic, :], xq_s[s * T : (s + 1) * T, ic * 128 : (ic + 1) * 128], transpose=True)
        c_xqf = C["in"]
        c_hsp = [0] * NTT
        for tt in range(NTT):
            gtt = s * NTT + tt
            hpar = tt % 2
            S.wait("vector", "act", prev_sil_last)
            c_gx = S.put("vector", _tt, col(48), rsx_cols[:, gtt : gtt + 1], mw_rep[:, 0:1], ALU.mult, inc="dve")
            c_ux = S.put("vector", _tt, col(52), rsx_cols[:, gtt : gtt + 1], mw_rep[:, 1:2], ALU.mult, inc="dve")
            if s == 1:
                for kk in (2 * tt, 2 * tt + 1):
                    S.wait("sync", "g", c_hq_spill[0])
                    if kk >= 2:
                        S.wait("sync", "g", rt0_resp[kk - 2])
                    rt0_loads[kk] = S.dma("sync", "in", hqfb[:, kk % 2, :], hq_s[0:T, kk * 128 : (kk + 1) * 128], transpose=True)
                    if kk == 0:
                        rt0_resp = [0] * KC
                    S.wait("gpsimd", "in", rt0_loads[kk])
                    rt0_resp[kk] = S.dma("gpsimd", "g", hq_s2[kk, :, 0:T], hqfb[:, kk % 2, :])
                if tt == NTT - 1:
                    c_rt2[0] = rt0_resp[KC - 1]
            for og in range(IC):
                opar = og % 4
                gbank = pb[og % 2][:] if og % 4 < 2 else pbig[0][:, (og % 2) * 512 : (og % 2) * 512 + 512]
                ubank = pb[2 + og % 2][:] if og % 4 < 2 else pbig[1][:, (og % 2) * 512 : (og % 2) * 512 + 512]
                osl = slice(og * 512, og * 512 + 512)
                if tt == 0 and og == 0:
                    S.wait("tensor", "in", max(c_wld, c_xqf))
                pidx = og - 4 if og >= 4 else IC - 4 + og
                S.wait("tensor", "act", c_us[pidx])
                cg = cu = 0
                for ic in range(IC):
                    incv = "pe" if ic == IC - 1 else None
                    r = S.put("tensor", _mm, gbank, xqf3[:, ic, tt * 128 : (tt + 1) * 128], wgq[:, ic, osl], ic == 0, ic == IC - 1, None, inc=incv)
                    if incv:
                        cg = r
                for ic in range(IC):
                    incv = "pe" if ic == IC - 1 else None
                    r = S.put("tensor", _mm, ubank, xqf3[:, ic, tt * 128 : (tt + 1) * 128], wuq[:, ic, osl], ic == 0, ic == IC - 1, None, inc=incv)
                    if incv:
                        cu = r
                c_lastmm = cu
                S.wait("scalar", "pe", cg)
                S.wait("scalar", "dve", c_gx)
                S.wait("scalar", "pool", c_h[pidx])
                c_sil[og] = S.put("scalar", _act, sgus[:, opar, :], gbank, AF.Silu, ccol(11), col(48), inc="act")
                S.wait("scalar", "pe", cu)
                S.wait("scalar", "dve", c_ux)
                c_us[og] = S.put("scalar", _act, sgus[:, 4 + opar, :], ubank, AF.Identity, ccol(11), col(52), inc="act")
                S.wait("gpsimd", "act", c_us[og])
                c_h[og] = S.put("gpsimd", _tt, h2[:, hpar, osl], sgus[:, opar, :], sgus[:, 4 + opar, :], ALU.mult, inc="pool")
            prev_sil_last = c_us[IC - 1]
            S.wait("vector", "pool", c_h[IC - 1])
            e = S.put("vector", _red, col(56), h2[:, hpar, :], ALU.max, True, inc="dve")
            S.wait("vector", "dve", e)
            e = S.put("vector", _ts, col(57), col(56), 1e-5, None, ALU.max, None, inc="dve")
            S.wait("vector", "dve", e)
            e = S.put("vector", _recip, col(58), col(57), inc="dve")
            S.wait("vector", "dve", e)
            e = S.put("vector", _ts, col(59), col(58), 127.0, None, ALU.mult, None, inc="dve")
            S.wait("vector", "dve", e)
            e = S.put("vector", _ts, sc_cols[:, gtt : gtt + 1], col(57), mw_rep[:, 2:3], 1.0 / 127.0, ALU.mult, ALU.mult, inc="dve")
            S.wait("vector", "dve", e)
            e = S.put("vector", _ts, h2[:, hpar, :], h2[:, hpar, :], col(59), MAGIC, ALU.mult, ALU.add, inc="dve")
            S.wait("vector", "dve", e)
            if tt >= 1:
                S.wait("vector", "g", c_hsp[tt - 1])
            chq = S.put("vector", _ts, hqst[:, 0, :], h2[:, hpar, :], -MAGIC, None, ALU.add, None, inc="dve")
            S.wait("gpsimd", "dve", chq)
            c_hsp[tt] = S.dma("gpsimd", "g", hq_s[gtt * 128 : gtt * 128 + 128, :], hqst[:, 0, :])
        c_hq_spill[s] = c_hsp[NTT - 1]
    S.wait("sync", "pe", c_lastmm)
    S.dma("sync", "in", wdq[:], wdq_s.rearrange("(c k) m -> k c m", c=KC))
    c_wdq = C["in"]
    S.flush()

    hq_r = hq_s2.rearrange("k p t -> p k t")
    rt1_loads = [0] * KC
    rt1_resp = [0] * KC
    c_dmm = [0] * 32
    c_ffn = [0] * 32
    c_sum = [0] * 32
    c_at = [0] * 32
    for gtt in range(32):
        s2 = gtt // NTT
        tt = gtt % NTT
        par = gtt % 2
        par3 = gtt % 3
        if gtt < NTT:
            for kk in (2 * gtt, 2 * gtt + 1):
                S.wait("sync", "g", c_hq_spill[1])
                if kk >= 4:
                    S.wait("sync", "g", rt1_resp[kk - 4])
                rt1_loads[kk] = S.dma("sync", "in", hqfb[:, kk % 4, :], hq_s[T : 2 * T, kk * 128 : (kk + 1) * 128], transpose=True)
                S.wait("gpsimd", "in", rt1_loads[kk])
                rt1_resp[kk] = S.dma("gpsimd", "g", hq_s2[kk, :, T : 2 * T], hqfb[:, kk % 4, :])
                if kk == KC - 1:
                    c_rt2[1] = rt1_resp[kk]
        S.wait("sync", "g", c_rt2[s2])
        if gtt >= 3:
            S.wait("sync", "pe", c_dmm[gtt - 3])
            S.wait("sync", "pool", c_sum[gtt - 3])
        S.dma("sync", "in", hqt[:, par3, :, :], hq_r[:, :, gtt * 128 : gtt * 128 + 128])
        S.dma("sync", "in", x3[:, par3, :], x_d[gtt * 128 : gtt * 128 + 128, :])
        cin = S.dma("sync", "in", Ore[:, par3, :], O_s[s2 * 1040 : (s2 + 1) * 1040, tt * 128 : (tt + 1) * 128], transpose=True)
        S.wait("tensor", "in", max(cin, c_wdq))
        if gtt >= 2:
            S.wait("tensor", "act", c_ffn[gtt - 2])
        for dc in range(2):
            dsl = slice(dc * 512, dc * 512 + 512)
            for kc in range(KC):
                incv = "pe" if (kc == KC - 1 and dc == 1) else None
                r = S.put("tensor", _mm, pb[dc * 2 + par][:], hqt[:, par3, kc, :], wdq[:, kc, dsl], kc == 0, kc == KC - 1, None, inc=incv)
                if incv:
                    c_dmm[gtt] = r
        S.wait("scalar", "pe", c_dmm[gtt])
        if gtt >= 1:
            S.wait("scalar", "pool", c_sum[gtt - 1])
        S.put("scalar", _act, ffn[:, 0:512], pb[par][:], AF.Identity, ccol(11), sc_cols[:, gtt : gtt + 1])
        c_ffn[gtt] = S.put("scalar", _act, ffn[:, 512:1024], pb[2 + par][:], AF.Identity, ccol(11), sc_cols[:, gtt : gtt + 1], inc="act")
        S.wait("vector", "in", cin)
        if gtt >= 1:
            S.wait("vector", "pool", c_sum[gtt - 1])
            S.wait("vector", "dve", c_at[gtt - 1])
        e = S.put("vector", _recip, rl16[:], Ore[:, par3, 1024:1040], inc="dve")
        S.wait("vector", "dve", e)
        c_at[gtt] = S.put(
            "vector", _tt,
            attn.rearrange("p (h d) -> p h d", h=16),
            Ore[:, par3, 0:1024].rearrange("p (h d) -> p h d", h=16),
            rl16.rearrange("p (h o) -> p h o", o=1).broadcast_to([128, 16, 64]),
            ALU.mult,
            inc="dve",
        )
        S.wait("gpsimd", "dve", c_at[gtt])
        S.wait("gpsimd", "act", c_ffn[gtt])
        e = S.put("gpsimd", _tt, sum1[:], attn[:], ffn[:], ALU.add, inc="pool")
        S.wait("gpsimd", "pool", e)
        c_sum[gtt] = S.put("gpsimd", _tt, outsb[:, par, :], sum1[:], x3[:, par3, :], ALU.add, inc="pool")
        S.wait("gpsimd", "pool", c_sum[gtt])
        S.dma("gpsimd", "st", out_d[gtt * 128 : gtt * 128 + 128, :], outsb[:, par, :])
    S.wait("sync", "st", C["st"])
    S.flush()
    es.close()
    return nc


_NC_CACHE = {}


def _get_nc():
    if "nc" not in _NC_CACHE:
        _NC_CACHE["nc"] = build_program()
    return _NC_CACHE["nc"]


def _host_consts():
    cols = np.zeros((128, 16), dtype=np.float32)
    vals = [MAGIC, -MAGIC, EXPB, 0.125, 1.0 / 1024.0, 1e-5, 127.0, 1.0 / 127.0,
            -0.5, 1.5, 1.0, 0.0]
    for j, v in enumerate(vals):
        cols[:, j] = v
    permm = np.zeros((128, 128), dtype=np.float32)
    for k in range(128):
        permm[k, k ^ 1] = 1.0
    inv = 10000.0 ** (-np.arange(0, 64, 2, dtype=np.float64) / 64.0)
    t = np.arange(T, dtype=np.float64)
    f = np.outer(t, inv)
    cosf = np.cos(f)
    sinf = np.sin(f)
    Cm = np.zeros((128, T), dtype=np.float64)
    Dm = np.zeros((128, T), dtype=np.float64)
    for p in range(128):
        d = p % 64
        j = d // 2
        Cm[p] = cosf[:, j]
        Dm[p] = sinf[:, j] if (d % 2 == 1) else -sinf[:, j]
    bf = ml_dtypes.bfloat16
    return (cols, permm.astype(bf), Cm.astype(np.float32).astype(bf),
            Dm.astype(np.float32).astype(bf))


def kernel(L0_attn_x, L0_attn_wq, L0_attn_wk, L0_attn_wv,
           L0_ffn_w_gate, L0_ffn_w_up, L0_ffn_w_down):
    x = np.ascontiguousarray(np.asarray(L0_attn_x, dtype=np.float32))
    wqT = np.ascontiguousarray(np.asarray(L0_attn_wq, dtype=np.float32).T)
    wkT = np.ascontiguousarray(np.asarray(L0_attn_wk, dtype=np.float32).T)
    wvT = np.ascontiguousarray(np.asarray(L0_attn_wv, dtype=np.float32).T)
    wgT = np.ascontiguousarray(np.asarray(L0_ffn_w_gate, dtype=np.float32).T)
    wuT = np.ascontiguousarray(np.asarray(L0_ffn_w_up, dtype=np.float32).T)
    wdT = np.ascontiguousarray(np.asarray(L0_ffn_w_down, dtype=np.float32).T)
    cols, permm, Cm, Dm = _host_consts()

    nc = _get_nc()
    in_maps = []
    for c in range(8):
        shard = np.ascontiguousarray(x[2 * c : 2 * c + 2].reshape(2 * T, D))
        in_maps.append({
            "x": shard, "wqT": wqT, "wkT": wkT, "wvT": wvT,
            "wgT": wgT, "wuT": wuT, "wdT": wdT,
            "cc": cols, "perm": permm, "ropeC": Cm, "ropeD": Dm,
        })
    res = run_bass_kernel_spmd(nc, in_maps, core_ids=list(range(8)))
    out = np.empty((16, T, D), dtype=np.float32)
    for c in range(8):
        out[2 * c : 2 * c + 2] = np.asarray(res.results[c]["out"]).reshape(2, T, D)
    return out
